# version 30
# baseline (speedup 1.0000x reference)
import sys
sys.path.insert(0, '/opt/trn_rl_repo')
import numpy as np
import ml_dtypes
import concourse.bass as bass
import concourse.mybir as mybir
from concourse.bass_utils import run_bass_kernel_spmd

F32 = mybir.dt.float32
BF16 = mybir.dt.bfloat16
AF = mybir.ActivationFunctionType
ALU = mybir.AluOpType
AX = mybir.AxisListType

D = 2048
T = 1024
NC = 8
NS = 4
EPS = 1e-5

PAIRED = [0, 4, 8, 1, 5, 9, 2, 6, 10, 3, 7, 11, 12, 13, 14, 15]


class Builder:
    def __init__(self):
        self.nc = bass.Bass("TRN2", target_bir_lowering=False)
        self.th = {k: [] for k in ("sync", "gpsimd", "tensor", "vector", "scalar")}
        self.sems = {}
        self.last_wait = {k: {} for k in self.th}
        self.res = {}

    def sem(self, name):
        if name not in self.sems:
            self.sems[name] = dict(n=0, h=None)
        return self.sems[name]

    def wait(self, eng, semname, val):
        if val <= 0:
            return
        if self.last_wait[eng].get(semname, 0) >= val:
            return
        self.last_wait[eng][semname] = val
        s = self.sem(semname)
        self.th[eng].append(lambda E, s=s, v=val: E.wait_ge(s["h"], v))

    ESEM = {"vector": "s_v", "scalar": "s_sc"}

    def emit(self, eng, fn):
        if eng in self.ESEM:
            return self.inc(eng, fn, None)
        self.th[eng].append(fn)
        return None

    def inc(self, eng, fn, semname):
        if eng in self.ESEM:
            semname = self.ESEM[eng]
            s = self.sem(semname)
            if s["n"] >= 1:
                self.wait(eng, semname, s["n"])
        else:
            s = self.sem(semname)
        s["n"] += 1
        v = s["n"]
        self.th[eng].append(lambda E, f=fn, s=s: f(E).then_inc(s["h"], 1))
        return v

    def nop_inc(self, eng, semname):
        s = self.sem(semname)
        s["n"] += 1
        v = s["n"]
        if eng == "tensor":
            self.th[eng].append(lambda E, s=s: E.nop().then_inc(s["h"], 1))
        else:
            self.th[eng].append(lambda E, s=s: E.engine_nop().then_inc(s["h"], 1))
        return v

    def dma(self, eng, semname, out_ap, in_ap):
        s = self.sem(semname)
        s["n"] += 16
        v = s["n"]
        self.th[eng].append(
            lambda E, o=out_ap, i=in_ap, s=s: E.dma_start(o, i).then_inc(s["h"], 16))
        return v

    def use(self, eng, resname):
        r = self.res.get(resname)
        if r is not None:
            self.wait(eng, r[0], r[1])

    def release(self, resname, semname, val):
        self.res[resname] = (semname, val)


def build_program(stage=4):
    B = Builder()
    nc = B.nc
    dt = nc.dram_tensor

    xT_d = dt("xT", [D, T], F32, kind="ExternalInput").ap()
    wsA_d = dt("wsA", [128, D, 128], F32, kind="ExternalInput").ap()
    wsB_d = dt("wsB", [128, 1024, 128], F32, kind="ExternalInput").ap()
    tbl_d = dt("tbl", [2, 512, T], BF16, kind="ExternalInput").ap()
    wg_d = dt("wg", [D, 8], F32, kind="ExternalInput").ap()
    smallbf_d = dt("smallbf", [128, 272], BF16, kind="ExternalInput").ap()
    smallf32_d = dt("smallf32", [128, 80], F32, kind="ExternalInput").ap()
    y_d = dt("y", [D, T], F32, kind="ExternalOutput").ap()
    cc_in = dt("cc_in", [2, D], F32, kind="Internal")
    cc_out = dt("cc_out", [2, D], F32, kind="Internal")
    mbounce = dt("mbounce", [8, T], BF16, kind="Internal")

    from contextlib import ExitStack
    ctx = ExitStack()
    sb = lambda name, shape, dtp: ctx.enter_context(nc.sbuf_tensor(name, shape, dtp))
    ps = lambda name, shape, dtp: ctx.enter_context(nc.psum_tensor(name, shape, dtp))

    acc = sb("accb", [128, 16 * T], F32)
    xnT = sb("xnT", [128, 16 * T], BF16)
    qt = sb("qt", [128, 16 * T], BF16)
    kt = sb("kt", [128, 16 * T], BF16)
    wring = sb("wring", [128, NS * 2048], BF16)
    tbl = sb("tblb", [128, 2 * 2048], BF16)
    maskB = sb("maskB", [128, 2048], BF16)
    sc1 = sb("sc1", [128, T], F32)
    sc2 = sb("sc2", [128, T], BF16)
    bfs = sb("bfs", [128, T], BF16)
    rows = sb("rows", [1, T], BF16)
    smallf32 = sb("smallf32b", [128, 80], F32)
    biascol = smallf32[:, 16:80]
    maskpair = sb("maskpair", [1, 1024], BF16)
    maskT = sb("maskT", [8, T], BF16)
    gatesb = sb("gatesb", [128, 64], F32)
    cntsb = sb("cntsb", [128, 64], F32)
    masksb = sb("masksb", [128, 64], BF16)
    Asb = sb("Asb", [128, 16], F32)
    A0 = sb("A0", [128, 16], F32)
    A1 = sb("A1", [128, 16], F32)
    A2a = sb("A2a", [128, 16], F32)
    A2b = sb("A2b", [128, 16], F32)
    A2 = sb("A2", [128, 16], F32)
    smallbf = sb("smallbfb", [128, 272], BF16)
    eye = smallbf[:, 0:128]
    onescol = smallbf[:, 128:129]
    onesrow = smallbf[0:1, 144:272]
    consts = smallf32[0:1, 0:8]
    bsel0 = smallf32[:, 8:9]
    bsel1 = smallf32[:, 9:10]
    wg_bf = sb("wgbf", [128, 128], BF16)

    pA = ps("pA", [128, T], F32)
    pB = ps("pB", [128, T], F32)
    pC = ps("pC", [128, T], F32)
    pD = ps("pD", [128, 512], F32)
    pT = ps("pT", [128, T], BF16)
    PS = {"pA": pA, "pB": pB, "pC": pC}

    MM = lambda out, l, r, st, sp: (lambda E: E.matmul(
        out, l, r, start=st, stop=sp, skip_group_check=True))

    slab_event = {}
    bias_event = {}
    tbl_event = {}
    slab_ld = {}
    bias_ld = {}
    tbl_ld = {}

    B.dma("sync", "s_small", smallbf[:, :], smallbf_d)
    B.dma("sync", "s_small", smallf32[:, :], smallf32_d)
    n_small = B.sem("s_small")["n"]
    for q4 in range(8):
        eng = "sync" if q4 in (5, 7) else "gpsimd"
        B.dma(eng, f"s_xc{q4}",
              acc[:, q4 * 2 * T:(q4 + 1) * 2 * T].rearrange("p (j n) -> p j n", j=2),
              xT_d[q4 * 256:(q4 + 1) * 256, :].rearrange("(j p) n -> p j n", p=128))
    B.dma("gpsimd", "s_wg", wg_bf[:, :].rearrange("p (c e) -> p c e", c=16),
          wg_d.rearrange("(c p) e -> p c e", p=128))

    slab_plan = []
    for t in range(48):
        slab_plan.append(("A", t))
    for o in range(16):
        slab_plan.append(("A", 48 + o))
    for p in range(4):
        for h in range(16):
            slab_plan.append(("A", 64 + 16 * p + h))
        for g in range(32):
            slab_plan.append(("B2", 32 * p + g))
    NSLAB = len(slab_plan)

    def slab_slot_ap(k):
        if k < 24 or k >= 64:
            s = k % 8 if k < 24 else (k - 64) % 8
            if s < 4:
                return s, wring[:, s * 2048:(s + 1) * 2048]
            return s, qt[:, (s - 4) * 2048:(s - 3) * 2048]
        s = k % NS
        return s, wring[:, s * 2048:(s + 1) * 2048]

    def emit_slab_dma(k):
        kind, si = slab_plan[k]
        slot, dstfull = slab_slot_ap(k)
        if k < 28 or k >= 72:
            prev = slab_event.get(k - 8)
        else:
            prev = slab_event.get(k - NS)
        if 68 <= k < 72:
            prev = ("s_twout", 16)
        if prev is not None:
            B.wait("gpsimd", prev[0], prev[1])
        if kind == "A":
            src = wsA_d[si].rearrange("(c p) q -> p c q", p=128)
            dst = dstfull.rearrange("p (c q) -> p c q", c=16)
        else:
            src = wsB_d[si].rearrange("(c p) q -> p c q", p=128)
            dst = dstfull[:, 0:1024].rearrange("p (c q) -> p c q", c=8)
        slab_ld[k] = (f"s_ws{slot}", B.dma("gpsimd", f"s_ws{slot}", dst, src))

    gp_slab_k = [0]

    def pump_slabs(upto_slab=None, upto_bias=None):
        if upto_slab is None:
            upto_slab = NSLAB
        while gp_slab_k[0] < upto_slab:
            emit_slab_dma(gp_slab_k[0]); gp_slab_k[0] += 1

    pump_slabs(upto_slab=8)

    def emit_tbl(q):
        j = q % 4
        slot = q % 2
        prev = tbl_event.get(q - 2)
        if prev is not None:
            B.wait("sync", prev[0], prev[1])
        B.dma("sync", f"s_tb{slot}", tbl[:, slot * 2048: slot * 2048 + 1024],
              tbl_d[0, 128 * j:128 * (j + 1), :])
        tbl_ld[q] = (f"s_tb{slot}",
                     B.dma("sync", f"s_tb{slot}",
                           tbl[:, slot * 2048 + 1024:(slot + 1) * 2048],
                           tbl_d[1, 128 * j:128 * (j + 1), :]))
    emit_tbl(0)
    emit_tbl(1)

    B.wait("scalar", "s_small", n_small)
    B.emit("scalar", lambda E: E.activation(consts[0:1, 3:4], consts[0:1, 3:4], AF.Ln))

    sqbuf = [bfs[:, :], maskB[:, 0:1024], maskB[:, 1024:2048]]
    B.wait("tensor", "s_small", n_small)
    for j in range(16):
        B.wait("vector", f"s_xc{j // 2}", 16)
        evcp = B.emit("vector", lambda E, j=j: E.tensor_copy(
            xnT[:, j * T:(j + 1) * T], acc[:, j * T:(j + 1) * T]))
        if j > 2:
            B.wait("vector", "s_tst", j - 2)
        ev = B.inc("vector", lambda E, j=j: E.tensor_tensor(
            sqbuf[j % 3], xnT[:, j * T:(j + 1) * T], xnT[:, j * T:(j + 1) * T],
            ALU.mult), "s_vsq")
        B.wait("tensor", "s_v", evcp)
        for h in (0, 512):
            B.emit("tensor", MM(pA[0:1, h:h + 512], onescol[:, 0:1],
                                xnT[:, j * T + h: j * T + h + 512], j == 0, j == 15))
        B.wait("tensor", "s_v", ev)
        B.emit("tensor", MM(pB[0:1, 0:512], onescol[:, 0:1], sqbuf[j % 3][:, 0:512],
                            j == 0, j == 15))
        B.inc("tensor", MM(pB[0:1, 512:1024], onescol[:, 0:1], sqbuf[j % 3][:, 512:1024],
                           j == 0, j == 15), "s_tst")
    v1 = sc1[0:1, :]
    vb = sc2[0:1, :]
    B.wait("vector", "s_tst", 16)
    B.emit("vector", lambda E: E.tensor_scalar(v1, pA[0:1, :], 1.0 / D, None, ALU.mult))
    evm = B.inc("vector", lambda E: E.tensor_copy(rows[0:1, :], v1), "s_vrows")
    B.wait("tensor", "s_v", evm)
    B.emit("tensor", MM(pA[:, 0:512], onesrow[0:1, :], rows[0:1, 0:512], True, True))
    evbm = B.inc("tensor", MM(pA[:, 512:1024], onesrow[0:1, :], rows[0:1, 512:1024],
                              True, True), "s_tbc")
    B.emit("vector", lambda E: E.tensor_scalar(vb, pB[0:1, :], 1.0 / D, None, ALU.mult))
    B.emit("vector", lambda E: E.tensor_tensor(v1, v1, v1, ALU.mult))
    evv = B.inc("vector", lambda E: E.tensor_tensor(v1, vb, v1, ALU.subtract), "s_vrows")
    B.wait("scalar", "s_v", evv)
    B.emit("scalar", lambda E: E.activation(v1, v1, AF.Ln, bias=consts[0:1, 0:1]))
    evsq = B.inc("scalar", lambda E: E.activation(v1, v1, AF.Exp,
                                                  scale=consts[0:1, 4:5]), "s_scrow")
    B.wait("vector", "s_sc", evsq)
    B.wait("vector", "s_tbc", evbm)
    evr = B.inc("vector", lambda E: E.tensor_copy(rows[0:1, :], v1), "s_vrows")
    B.wait("tensor", "s_v", evr)
    B.emit("tensor", MM(pB[:, 0:512], onesrow[0:1, :], rows[0:1, 0:512], True, True))
    evbr = B.inc("tensor", MM(pB[:, 512:1024], onesrow[0:1, :], rows[0:1, 512:1024],
                              True, True), "s_tbc")
    B.wait("vector", "s_tbc", evbr)
    mBbf = bfs[:, :]
    rBbf = maskB[:, 0:1024]
    B.emit("vector", lambda E: E.tensor_copy(mBbf, pA[:, :]))
    evev = B.inc("vector", lambda E: E.tensor_copy(rBbf, pB[:, :]), "s_vlnc")
    B.release("pA", "s_v", evev)
    B.release("pB", "s_v", evev)
    evlnc_tile = []
    for j in range(16):
        xj = xnT[:, j * T:(j + 1) * T]
        B.emit("vector", lambda E, xj=xj: E.tensor_tensor(xj, xj, mBbf, ALU.subtract))
        evlnc_tile.append(B.inc("vector", lambda E, xj=xj: E.tensor_tensor(
            xj, xj, rBbf, ALU.mult), "s_vlnc"))
    evlnc = evlnc_tile[-1]
    evlnc2 = None

    def emit_pass2(j):
        nonlocal evlnc2
        B.wait("vector", "s_xc6", 16)
        B.wait("vector", "s_xc7", 16)
        aj = acc[:, j * T:(j + 1) * T]
        xj = xnT[:, j * T:(j + 1) * T]
        evlnc2 = B.inc("vector", lambda E, aj=aj, xj=xj: E.tensor_tensor(
            aj, aj, xj, ALU.add), "s_vlnc")

    def _flush_and_return():
        for o in range(16):
            B.dma("sync", "s_out", y_d[o * 128:(o + 1) * 128, :],
                  acc[:, o * T:(o + 1) * T])
        B.wait("sync", "s_out", 16 * 16)
        _materialize(B, nc, ctx)
        return nc

    if stage <= 1:
        for j in range(16):
            emit_pass2(j)
        B.wait("sync", "s_v", evlnc2)
        return _flush_and_return()

    stream = [("K", t) for t in PAIRED] + [("V", t) for t in range(16)] + \
             [("Q", t) for t in PAIRED]

    def qkv_group(idx, slotname):
        slot = PS[slotname]
        k = idx
        B.wait("tensor", *slab_ld[k])
        B.use("tensor", slotname)
        ws = slab_slot_ap(k)[1]
        ev = None
        for d in range(16):
            if idx < 2:
                B.wait("tensor", "s_v", evlnc_tile[d])
            for h in (0, 512):
                fn = MM(slot[:, h:h + 512], ws[:, d * 128:(d + 1) * 128],
                        xnT[:, d * T + h: d * T + h + 512], d == 0, d == 15)
                if d == 15 and h == 512:
                    ev = B.inc("tensor", fn, "s_tqkv")
                else:
                    B.emit("tensor", fn)
        slab_event[k] = ("s_tqkv", ev)
        pump_slabs(upto_slab=min(k + 1 + (8 if (k >= 64 or k < 20) else NS), NSLAB))
        return ev

    sc1b = sc1[:, :].bitcast(BF16)

    def rope_pair(block, j, se, so, ev_odd, dest, tblq, ke, ko):
        tslot = tblq % 2
        cosb = tbl[:, tslot * 2048: tslot * 2048 + 1024]
        sinb = tbl[:, tslot * 2048 + 1024:(tslot + 1) * 2048]
        pse, pso = PS[se], PS[so]
        dE = dest[:, j * T:(j + 1) * T]
        dO = dest[:, (4 + j) * T:(5 + j) * T]
        B.wait("scalar", "s_tqkv", ev_odd)
        ru = B.inc("scalar", lambda E, dE=dE: E.activation(
            dE, pse[:, :], AF.Identity, bias=biascol[:, ke:ke + 1]), "s_scp")
        rv = B.inc("scalar", lambda E, dO=dO: E.activation(
            dO, pso[:, :], AF.Identity, bias=biascol[:, ko:ko + 1]), "s_scp")
        B.release(se, "s_sc", ru)
        B.release(so, "s_sc", rv)
        x1 = sc1b[:, 0:1024]
        x2 = sc1b[:, 1024:2048]
        x3 = sc2[:, :]
        B.wait("vector", "s_sc", rv)
        B.wait("vector", *tbl_ld[tblq])
        B.emit("vector", lambda E, dE=dE: E.tensor_tensor(x1, dE, cosb, ALU.mult))
        B.emit("vector", lambda E, dO=dO: E.tensor_tensor(x2, dO, sinb, ALU.mult))
        B.emit("vector", lambda E, dO=dO: E.tensor_tensor(x3, dO, cosb, ALU.mult))
        B.emit("vector", lambda E, dE=dE, dO=dO: E.tensor_tensor(dO, dE, sinb, ALU.mult))
        B.emit("vector", lambda E, dE=dE: E.tensor_tensor(dE, x1, x2, ALU.subtract))
        r1 = B.inc("vector", lambda E, dO=dO: E.tensor_tensor(dO, x3, dO, ALU.add),
                   "s_vqkv")
        tbl_event[tblq] = ("s_v", r1)
        if tblq + 2 <= 7:
            emit_tbl(tblq + 2)
        return r1

    ringK = ["pC", "pA", "pB"]
    pend = {}
    kpair_done_ev = {}
    sq_emitted = [0]
    sq_ready = {}
    sq_ev = {}

    sq_pos = {}

    def emit_kk_squares(pos):
        while sq_emitted[0] in sq_ready:
            j = sq_emitted[0]
            B.wait("vector", *sq_ready[j])
            if j > 2:
                B.wait("vector", "s_tnsq", j - 2)
            sq_ev[j] = B.inc("vector", lambda E, j=j: E.tensor_tensor(
                sqbuf[j % 3], kt[:, j * T:(j + 1) * T], kt[:, j * T:(j + 1) * T],
                ALU.mult), "s_vsq")
            sq_pos[j] = pos
            sq_emitted[0] += 1

    nsq_emitted = [0]
    nsq_used_pB = [False]

    def emit_nsq(limit=16, pos=99):
        while nsq_emitted[0] < limit and nsq_emitted[0] in sq_ev \
                and sq_pos.get(nsq_emitted[0], 99) <= pos - 2:
            j = nsq_emitted[0]
            if not nsq_used_pB[0]:
                B.use("tensor", "pB")
                nsq_used_pB[0] = True
            B.wait("tensor", "s_v", sq_ev[j])
            B.emit("tensor", MM(pB[0:1, 0:512], onescol[:, 0:1], sqbuf[j % 3][:, 0:512],
                                j == 0, j == 15))
            B.inc("tensor", MM(pB[0:1, 512:1024], onescol[:, 0:1],
                               sqbuf[j % 3][:, 512:1024], j == 0, j == 15), "s_tnsq")
            nsq_emitted[0] += 1

    for i in range(16):
        blk, t = stream[i]
        slotname = ringK[i % 3] if i < 8 else ["pC", "pA"][i % 2]
        ev = qkv_group(i, slotname)
        if t < 4:
            pend[t] = (slotname, ev, i)
        elif t < 8:
            jj = t - 4
            se, ev_e, ke = pend.pop(jj)
            kpair_done_ev[jj] = rope_pair("K", jj, se, slotname, ev, kt, jj, ke, i)
            sq_ready[jj] = ("s_v", kpair_done_ev[jj])
            sq_ready[4 + jj] = ("s_v", kpair_done_ev[jj])
        else:
            B.wait("scalar", "s_tqkv", ev)
            r = B.inc("scalar", lambda E, t=t, s=PS[slotname], i=i: E.activation(
                kt[:, t * T:(t + 1) * T], s[:, :], AF.Identity,
                bias=biascol[:, i:i + 1]), "s_scp")
            B.release(slotname, "s_sc", r)
            sq_ready[t] = ("s_sc", r)
        emit_kk_squares(i)
        if i >= 6:
            emit_nsq(pos=i)

    if stage == 11:
        B.wait("sync", "s_v", B.sem("s_v")["n"])
        B.wait("sync", "s_sc", B.sem("s_sc")["n"])
        return _flush_and_return()

    emit_kk_squares(16)
    assert sq_emitted[0] == 16
    emit_nsq()
    assert nsq_emitted[0] == 16
    B.wait("scalar", "s_tnsq", 16)
    B.emit("scalar", lambda E: E.activation(sc1[0:1, :], pB[0:1, :], AF.Relu,
                                            bias=consts[0:1, 6:7]))
    B.emit("scalar", lambda E: E.activation(sc1[0:1, :], sc1[0:1, :], AF.Ln,
                                            bias=consts[0:1, 5:6]))
    evr2 = B.inc("scalar", lambda E: E.activation(rows[0:1, :], sc1[0:1, :], AF.Exp,
                                                  scale=consts[0:1, 4:5]), "s_scrow")
    B.wait("tensor", "s_sc", evr2)
    B.emit("tensor", MM(pB[:, 0:512], onesrow[0:1, :], rows[0:1, 0:512], True, True))
    evrn = B.inc("tensor", MM(pB[:, 512:1024], onesrow[0:1, :], rows[0:1, 512:1024],
                              True, True), "s_trn")

    if stage == 12:
        B.wait("sync", "s_trn", 1)
        return _flush_and_return()

    for i in range(16, 32):
        blk, t = stream[i]
        slotname = ["pC", "pA"][i % 2]
        ev = qkv_group(i, slotname)
        B.wait("vector", "s_tqkv", ev)
        B.wait("vector", "s_trn", evrn)
        slot = PS[slotname]
        B.emit("vector", lambda E, t=t: E.tensor_tensor(
            sc1[:, :], kt[:, t * T:(t + 1) * T], pB[:, :], ALU.mult))
        rp = B.inc("vector", lambda E, slot=slot, i=i: E.scalar_tensor_tensor(
            sc2[:, :], slot[:, :], biascol[:, i:i + 1], sc1[:, :], ALU.add, ALU.mult),
            "s_vqkv")
        r = B.inc("vector", lambda E, t=t: E.reduce_sum(
            Asb[:, t:t + 1], sc2[:, :], axis=AX.X), "s_vqkv")
        B.release(slotname, "s_v", rp)
        emit_pass2(i - 16)
    B.release("pB", "s_v", B.sem("s_v")["n"])

    if stage == 13:
        B.wait("sync", "s_v", B.sem("s_v")["n"])
        return _flush_and_return()

    B.emit("vector", lambda E: E.tensor_scalar(A0[:, :], Asb[:, :], bsel0[:, 0:1],
                                               None, ALU.mult))
    evA = B.inc("vector", lambda E: E.tensor_scalar(A1[:, :], Asb[:, :], bsel1[:, 0:1],
                                                    None, ALU.mult), "s_vA")
    B.wait("sync", "s_v", evA)
    B.dma("sync", "s_dda", cc_in.ap()[0:1, :].rearrange("o (p j) -> (o p) j", p=128), A0[:, :])
    B.dma("sync", "s_dda", cc_in.ap()[1:2, :].rearrange("o (p j) -> (o p) j", p=128), A1[:, :])

    def emit_collective():
        B.wait("gpsimd", "s_dda", 32)
        B.inc("gpsimd", lambda E: E.collective_compute(
            "AllReduce", ALU.add, replica_groups=[list(range(NC))],
            ins=[cc_in.ap().opt()], outs=[cc_out.ap().opt()]), "s_cc")
        B.wait("sync", "s_cc", 1)
        B.dma("sync", "s_a2",
              A2a[:, :], cc_out.ap()[0:1, :].rearrange("o (p j) -> (o p) j", p=128))
        B.dma("sync", "s_a2",
              A2b[:, :], cc_out.ap()[1:2, :].rearrange("o (p j) -> (o p) j", p=128))

    if stage == 14:
        emit_collective()
        B.wait("sync", "s_a2", 32)
        return _flush_and_return()

    pend = {}
    ringQ = ["pA", "pB", "pC"]
    ev_aqt = None
    a2_done = [False]
    aqt_q = []

    def emit_aqt(j, prod_sem, prod_val):
        nonlocal ev_aqt
        if not a2_done[0]:
            B.wait("vector", "s_a2", 32)
            B.emit("vector", lambda E: E.tensor_scalar(A2[:, :], A2a[:, :], bsel0[:, 0:1],
                                                       None, ALU.mult))
            evA2 = B.inc("vector", lambda E: E.scalar_tensor_tensor(
                A2[:, :], A2b[:, :], bsel1[:, 0:1], A2[:, :], ALU.mult, ALU.add), "s_va2c")
            B.wait("scalar", "s_v", evA2)
            a2_done[0] = True
        B.wait("scalar", prod_sem, prod_val)
        ev_aqt = B.inc("scalar", lambda E, j=j: E.activation(
            qt[:, j * T:(j + 1) * T], qt[:, j * T:(j + 1) * T], AF.Identity,
            scale=A2[:, j:j + 1]), "s_vaqt")

    def flush_aqt(pos):
        while aqt_q and aqt_q[0][3] <= pos - 2:
            j, sem, val, _ = aqt_q.pop(0)
            emit_aqt(j, sem, val)

    for i in range(32, 48):
        blk, t = stream[i]
        slotname = ringQ[(i - 32) % 3]
        ev = qkv_group(i, slotname)
        if t < 4:
            pend[t] = (slotname, ev, i)
        elif t < 8:
            jj = t - 4
            se, ev_e, ke = pend.pop(jj)
            r1 = rope_pair("Q", jj, se, slotname, ev, qt, 4 + jj, ke, i)
            aqt_q.append((jj, "s_v", r1, i))
            aqt_q.append((4 + jj, "s_v", r1, i))
        else:
            B.wait("scalar", "s_tqkv", ev)
            r = B.inc("scalar", lambda E, t=t, s=PS[slotname], i=i: E.activation(
                qt[:, t * T:(t + 1) * T], s[:, :], AF.Identity,
                bias=biascol[:, i:i + 1]), "s_scp")
            B.release(slotname, "s_sc", r)
            aqt_q.append((t, "s_sc", r, i))
        if i == 35:
            emit_collective()
        if i >= 44:
            flush_aqt(i)
    flush_aqt(99)

    if stage == 15:
        B.wait("sync", "s_v", B.sem("s_v")["n"])
        B.wait("sync", "s_sc", B.sem("s_sc")["n"])
        B.wait("sync", "s_a2", 32)
        return _flush_and_return()

    B.wait("tensor", "s_wg", 16)
    B.use("tensor", "pD")
    evg = None
    for nj in range(8):
        for d in range(16):
            fn = MM(pD[:, nj * 8:(nj + 1) * 8],
                    xnT[:, d * T + nj * 128: d * T + (nj + 1) * 128],
                    wg_bf[:, d * 8:(d + 1) * 8], d == 0, d == 15)
            if nj == 7 and d == 15:
                evg = B.inc("tensor", fn, "s_tgate")
            else:
                B.emit("tensor", fn)


    B.wait("vector", "s_tgate", evg)
    B.emit("vector", lambda E: E.tensor_copy(gatesb[:, :], pD[:, 0:64]))
    evgc = B.inc("vector", lambda E: E.memset(cntsb[:, :], 0.0), "s_vgate")
    B.release("pD", "s_v", evgc)
    V3 = lambda t, lo, w: t[:, 0:64].rearrange("p (nj e) -> p nj e", nj=8)[:, :, lo:lo + w]
    for k in range(1, 8):
        w = 8 - k
        B.emit("vector", lambda E, k=k, w=w: E.tensor_tensor(
            V3(sc1, 0, w), V3(gatesb, k, w), V3(gatesb, 0, w), ALU.is_gt))
        B.emit("vector", lambda E, k=k, w=w: E.tensor_tensor(
            V3(cntsb, 0, w), V3(cntsb, 0, w), V3(sc1, 0, w), ALU.add))
        B.emit("vector", lambda E, k=k, w=w: E.tensor_tensor(
            V3(sc1, 0, w), V3(gatesb, 0, w), V3(gatesb, k, w), ALU.is_ge))
        B.emit("vector", lambda E, k=k, w=w: E.tensor_tensor(
            V3(cntsb, k, w), V3(cntsb, k, w), V3(sc1, 0, w), ALU.add))
    evmask = B.inc("vector", lambda E: E.tensor_scalar(
        masksb[:, :], cntsb[:, :], 4.0, None, ALU.is_lt), "s_vgate")


    for o in range(16):
        k = 48 + o
        slotname = ringK[o % 3]
        slot = PS[slotname]
        B.wait("tensor", "s_sc", ev_aqt)
        B.wait("tensor", *slab_ld[k])
        B.use("tensor", slotname)
        ws = slab_slot_ap(k)[1]
        ev = None
        for d in range(16):
            for h in (0, 512):
                fn = MM(slot[:, h:h + 512], ws[:, d * 128:(d + 1) * 128],
                        qt[:, d * T + h: d * T + h + 512], d == 0, d == 15)
                if d == 15 and h == 512:
                    ev = B.inc("tensor", fn, "s_twout")
                else:
                    B.emit("tensor", fn)
        slab_event[k] = ("s_twout", ev)
        pump_slabs(upto_slab=min(k + 1 + (8 if (k >= 64 or k < 20) else NS), NSLAB))
        B.wait("vector", "s_twout", ev)
        aj = acc[:, o * T:(o + 1) * T]
        r = B.inc("vector", lambda E, aj=aj, slot=slot, k=k: E.scalar_tensor_tensor(
            aj, slot[:, :], biascol[:, k:k + 1], aj, ALU.add, ALU.add), "s_vwout")
        B.release(slotname, "s_v", r)
        ev_wout_last = r
        if o == 1:
            B.wait("tensor", "s_v", evmask)
            B.use("tensor", "pT")
            evmt = None
            for nj in range(8):
                fn = lambda E, nj=nj: E.transpose(
                    pT[0:8, nj * 128:(nj + 1) * 128], masksb[:, nj * 8:(nj + 1) * 8], eye[:, :])
                if nj < 7:
                    B.emit("tensor", fn)
                else:
                    evmt = B.inc("tensor", fn, "s_tmt")
            B.wait("vector", "s_tmt", evmt)
            evmTc = B.inc("vector", lambda E: E.tensor_copy(maskT[0:8, :], pT[0:8, :]), "s_vmaskT")
            B.release("pT", "s_v", evmTc)
            B.wait("sync", "s_v", evmTc)
            B.dma("sync", "s_ddm", mbounce.ap()[:, :], maskT[0:8, :])

    if stage <= 3:
        B.wait("sync", "s_v", ev_wout_last)
        return _flush_and_return()


    res_u = {"u0": ("pC", lambda: pC[:, 0:512]),
             "u1": ("pC", lambda: pC[:, 512:1024]),
             "u2": ("pD", lambda: pD[:, 0:512])}
    B.res["u0"] = B.res.get("pC")
    B.res["u1"] = B.res.get("pC")
    B.res["u2"] = B.res.get("pD")

    w1slot = ["pA", "pB"]
    hw_counter = [0]
    for p in range(4):
        mb_ev = [None, None]
        mb_step = [0]

        def emit_maskB_step(p=p, mb_ev=mb_ev, mb_step=mb_step):
            eL = mb_step[0]
            if eL >= 2:
                return
            B.wait("sync", "s_ddm", 16)
            if p > 0 or eL > 0:
                B.wait("sync", "s_tmb", 4 * p + 2 * eL)
            B.dma("sync", "s_mp", maskpair[0:1, :],
                  mbounce.ap()[2 * p + eL:2 * p + eL + 1, :])
            for hh in range(2):
                B.wait("tensor", "s_mp", 16 * (2 * p + eL + 1))
                B.use("tensor", "u2")
                evb = B.inc("tensor", MM(pD[:, 0:512], onesrow[0:1, :],
                                         maskpair[0:1, hh * 512: hh * 512 + 512],
                                         True, True), "s_tmb")
                B.wait("vector", "s_tmb", evb)
                r = B.inc("vector", lambda E, eL=eL, hh=hh: E.tensor_copy(
                    maskB[:, eL * 1024 + hh * 512: eL * 1024 + hh * 512 + 512],
                    pD[:, 0:512]), "s_vmb")
                B.release("u2", "s_v", r)
                mb_ev[eL] = r
            mb_step[0] += 1

        for eL in range(2):
            for h in range(8):
                hw = hw_counter[0]
                k = 64 + 48 * p + 8 * eL + h
                slotname = w1slot[hw % 2]
                slot = PS[slotname]
                B.wait("tensor", *slab_ld[k])
                B.use("tensor", slotname)
                ws = slab_slot_ap(k)[1]
                ev = None
                for hh in (0, 512):
                    for d in range(16):
                        fn = MM(slot[:, hh:hh + 512], ws[:, d * 128:(d + 1) * 128],
                                xnT[:, d * T + hh: d * T + hh + 512], d == 0, d == 15)
                        if d == 15 and hh == 512:
                            ev = B.inc("tensor", fn, "s_tw1")
                        else:
                            B.emit("tensor", fn)
                slab_event[k] = ("s_tw1", ev)
                pump_slabs(upto_slab=min(k + 1 + (8 if (k >= 64 or k < 20) else NS), NSLAB))
                if eL == 0 and h < 2:
                    emit_maskB_step()
                B.wait("vector", "s_tw1", ev)
                B.wait("vector", "s_v", mb_ev[eL])
                if p > 0:
                    B.wait("vector", "s_tw2", 32 * p)
                dst = kt[:, (eL * 8 + h) * T:(eL * 8 + h + 1) * T]
                r = B.inc("vector", lambda E, dst=dst, slot=slot, eL=eL: E.scalar_tensor_tensor(
                    dst, slot[:, :], 0.0, maskB[:, eL * 1024:(eL + 1) * 1024],
                    ALU.max, ALU.mult), "s_vhT")
                B.release(slotname, "s_v", r)
                ev_hT_last = r
                hw_counter[0] += 1
        for g in range(32):
            o, nh = g // 2, g % 2
            uname = ["u0", "u1", "u2"][g % 3]
            psname, uap = res_u[uname]
            k_e0 = 64 + 48 * p + 16 + 2 * o
            k_e1 = k_e0 + 1
            B.wait("tensor", *slab_ld[k_e0])
            B.wait("tensor", *slab_ld[k_e1])
            B.wait("tensor", "s_v", ev_hT_last)
            B.use("tensor", uname)
            first = True
            ev = None
            for eL, kk in ((0, k_e0), (1, k_e1)):
                ws = slab_slot_ap(kk)[1][:, 0:1024]
                for hc in range(8):
                    last = (eL == 1 and hc == 7)
                    fn = MM(uap()[:, :], ws[:, hc * 128:(hc + 1) * 128],
                            kt[:, (eL * 8 + hc) * T + nh * 512:
                               (eL * 8 + hc) * T + nh * 512 + 512],
                            first, last)
                    if last:
                        ev = B.inc("tensor", fn, "s_tw2")
                    else:
                        B.emit("tensor", fn)
                    first = False
            if nh == 1:
                slab_event[k_e0] = ("s_tw2", ev)
                slab_event[k_e1] = ("s_tw2", ev)
                pump_slabs(upto_slab=min(k_e1 + 1 + 8, NSLAB))
            B.wait("vector", "s_tw2", ev)
            aslice = acc[:, o * T + nh * 512: o * T + nh * 512 + 512]
            r = B.inc("vector", lambda E, aslice=aslice, uap=uap: E.tensor_tensor(
                aslice, aslice, uap()[:, :], ALU.add), "s_vw2")
            B.release(uname, "s_v", r)
            if p == 3 and nh == 1:
                B.wait("sync", "s_v", B.sem("s_v")["n"])
                B.dma("sync", "s_out", y_d[o * 128:(o + 1) * 128, :],
                      acc[:, o * T:(o + 1) * T])

    B.wait("sync", "s_out", 16 * 16)
    _materialize(B, nc, ctx)
    return nc


def _materialize(B, nc, ctx):
    from contextlib import ExitStack
    with ExitStack() as outer:
        for name, s in B.sems.items():
            s["h"] = outer.enter_context(nc.semaphore(name))
        block = outer.enter_context(nc.Block())

        @block.sync
        def _(E):
            for t in B.th["sync"]:
                t(E)

        @block.gpsimd
        def _(E):
            for t in B.th["gpsimd"]:
                t(E)

        @block.tensor
        def _(E):
            for t in B.th["tensor"]:
                t(E)

        @block.vector
        def _(E):
            for t in B.th["vector"]:
                t(E)

        @block.scalar
        def _(E):
            for t in B.th["scalar"]:
                t(E)


_NC_CACHE = []


def _get_program():
    if not _NC_CACHE:
        _NC_CACHE.append(build_program())
    return _NC_CACHE[0]


def _host_prep(L0_attn_X, L0_attn_W_in, L0_attn_b_in, L0_attn_W_out, L0_attn_b_out,
               L0_ffn_wg, L0_ffn_w1, L0_ffn_w2):
    f32 = np.float32
    bf = ml_dtypes.bfloat16
    sigma = np.concatenate([np.arange(0, 1024, 2), np.arange(1, 1024, 2),
                            np.arange(1024, 2048)])
    rows_order = np.concatenate([2048 + sigma, 4096 + sigma, sigma])

    Wnew = np.asarray(L0_attn_W_in, f32)[rows_order]
    bnew = np.asarray(L0_attn_b_in, f32)[rows_order]
    Wop = np.asarray(L0_attn_W_out, f32)[:, :]
    Wop = Wop[:, sigma]
    bout = np.asarray(L0_attn_b_out, f32)
    wg = np.ascontiguousarray(np.asarray(L0_ffn_wg, f32))
    w1 = np.asarray(L0_ffn_w1, f32)
    w2 = np.asarray(L0_ffn_w2, f32)

    stream_tiles = PAIRED + [16 + t for t in range(16)] + [32 + t for t in PAIRED]
    slabsA = np.empty((128, D, 128), f32)
    biasstream = np.empty((64, 128), f32)
    for i, t in enumerate(stream_tiles):
        slabsA[i] = Wnew[128 * t:128 * (t + 1), :].T
        biasstream[i] = bnew[128 * t:128 * (t + 1)]
    for o in range(16):
        slabsA[48 + o] = Wop[128 * o:128 * (o + 1), :].T
        biasstream[48 + o] = bout[128 * o:128 * (o + 1)]
    for p in range(4):
        for eL in range(2):
            e = 2 * p + eL
            for h in range(8):
                slabsA[64 + 16 * p + 8 * eL + h] = w1[e, 128 * h:128 * (h + 1), :].T
    slabsB = np.empty((128, 1024, 128), f32)
    for p in range(4):
        for o in range(16):
            for eL in range(2):
                e = 2 * p + eL
                slabsB[32 * p + 2 * o + eL] = w2[e, 128 * o:128 * (o + 1), :].T

    X = np.asarray(L0_attn_X, f32)
    inv = (10000.0 ** (-(np.arange(0, 1024, 2, dtype=np.float64)) / 1024.0))
    in_maps = []
    smallbf = np.zeros((128, 272), bf)
    smallbf[:, 0:128] = np.eye(128, dtype=bf)
    smallbf[:, 128] = bf(1)
    smallbf[0, 144:272] = bf(1)
    common = dict(wsA=slabsA, wsB=slabsB, wg=wg, smallbf=smallbf)
    for c in range(NC):
        b, q = c // 4, c % 4
        pos = np.arange(q * 1024, (q + 1) * 1024, dtype=np.float64)
        ang = pos[None, :] * inv[:, None]
        tblc = np.empty((2, 512, T), bf)
        tblc[0] = np.cos(ang).astype(np.float32).astype(bf)
        tblc[1] = np.sin(ang).astype(np.float32).astype(bf)
        m = dict(common)
        m["xT"] = np.ascontiguousarray(X[b, q * 1024:(q + 1) * 1024, :].T)
        m["tbl"] = tblc
        sf = np.zeros((128, 80), f32)
        sf[0, 0:8] = [EPS, 1e-12, 0.0, 1.0, -0.5, 1e-24, -1e-24, 0]
        sf[:, 8] = 1.0 if b == 0 else 0.0
        sf[:, 9] = 1.0 if b == 1 else 0.0
        sf[:, 16:80] = biasstream.T
        m["smallf32"] = sf
        in_maps.append(m)
    return in_maps


def kernel(**inputs):
    in_maps = _host_prep(**inputs)
    nc = _get_program()
    res = run_bass_kernel_spmd(nc, in_maps, core_ids=list(range(NC)))
    out = np.empty((2, 4096, D), np.float32)
    for c in range(NC):
        b, q = c // 4, c % 4
        out[b, q * 1024:(q + 1) * 1024, :] = res.results[c]["y"].T
    return out


if __name__ == "__main__":
    import reference as _ref
    inp = {k: np.asarray(v) for k, v in _ref.setup_inputs().items()}
    got = kernel(**inp)
    want = np.asarray(_ref.reference(**inp))
    num = float(((got - want) ** 2).mean())
    den = float((want ** 2).mean())
    print(f"resid_var_ratio={num / den:.3e} max_abs={np.max(np.abs(got - want)):.3e}")


# version 31
# speedup vs baseline: 1.0028x; 1.0028x over previous
import sys
sys.path.insert(0, '/opt/trn_rl_repo')
import numpy as np
import ml_dtypes
import concourse.bass as bass
import concourse.mybir as mybir
from concourse.bass_utils import run_bass_kernel_spmd

F32 = mybir.dt.float32
BF16 = mybir.dt.bfloat16
AF = mybir.ActivationFunctionType
ALU = mybir.AluOpType
AX = mybir.AxisListType

D = 2048
T = 1024
NC = 8
NS = 4
EPS = 1e-5

PAIRED = [0, 4, 8, 1, 5, 9, 2, 6, 10, 3, 7, 11, 12, 13, 14, 15]


class Builder:
    def __init__(self):
        self.nc = bass.Bass("TRN2", target_bir_lowering=False)
        self.th = {k: [] for k in ("sync", "gpsimd", "tensor", "vector", "scalar")}
        self.sems = {}
        self.last_wait = {k: {} for k in self.th}
        self.res = {}

    def sem(self, name):
        if name not in self.sems:
            self.sems[name] = dict(n=0, h=None)
        return self.sems[name]

    def wait(self, eng, semname, val):
        if val <= 0:
            return
        if self.last_wait[eng].get(semname, 0) >= val:
            return
        self.last_wait[eng][semname] = val
        s = self.sem(semname)
        self.th[eng].append(lambda E, s=s, v=val: E.wait_ge(s["h"], v))

    ESEM = {"vector": "s_v", "scalar": "s_sc"}

    def emit(self, eng, fn):
        if eng in self.ESEM:
            return self.inc(eng, fn, None)
        self.th[eng].append(fn)
        return None

    def inc(self, eng, fn, semname):
        if eng in self.ESEM:
            semname = self.ESEM[eng]
            s = self.sem(semname)
            if s["n"] >= 1:
                self.wait(eng, semname, s["n"])
        else:
            s = self.sem(semname)
        s["n"] += 1
        v = s["n"]
        self.th[eng].append(lambda E, f=fn, s=s: f(E).then_inc(s["h"], 1))
        return v

    def nop_inc(self, eng, semname):
        s = self.sem(semname)
        s["n"] += 1
        v = s["n"]
        if eng == "tensor":
            self.th[eng].append(lambda E, s=s: E.nop().then_inc(s["h"], 1))
        else:
            self.th[eng].append(lambda E, s=s: E.engine_nop().then_inc(s["h"], 1))
        return v

    def dma(self, eng, semname, out_ap, in_ap):
        s = self.sem(semname)
        s["n"] += 16
        v = s["n"]
        self.th[eng].append(
            lambda E, o=out_ap, i=in_ap, s=s: E.dma_start(o, i).then_inc(s["h"], 16))
        return v

    def use(self, eng, resname):
        r = self.res.get(resname)
        if r is not None:
            self.wait(eng, r[0], r[1])

    def release(self, resname, semname, val):
        self.res[resname] = (semname, val)


def build_program(stage=4):
    B = Builder()
    nc = B.nc
    dt = nc.dram_tensor

    xT_d = dt("xT", [D, T], F32, kind="ExternalInput").ap()
    wsA_d = dt("wsA", [128, D, 128], F32, kind="ExternalInput").ap()
    wsB_d = dt("wsB", [128, 1024, 128], F32, kind="ExternalInput").ap()
    tbl_d = dt("tbl", [2, 512, T], BF16, kind="ExternalInput").ap()
    wg_d = dt("wg", [D, 8], F32, kind="ExternalInput").ap()
    smallbf_d = dt("smallbf", [128, 272], BF16, kind="ExternalInput").ap()
    smallf32_d = dt("smallf32", [128, 80], F32, kind="ExternalInput").ap()
    y_d = dt("y", [D, T], F32, kind="ExternalOutput").ap()
    cc_in = dt("cc_in", [2, D], F32, kind="Internal")
    cc_out = dt("cc_out", [2, D], F32, kind="Internal")
    mbounce = dt("mbounce", [8, T], BF16, kind="Internal")

    from contextlib import ExitStack
    ctx = ExitStack()
    sb = lambda name, shape, dtp: ctx.enter_context(nc.sbuf_tensor(name, shape, dtp))
    ps = lambda name, shape, dtp: ctx.enter_context(nc.psum_tensor(name, shape, dtp))

    acc = sb("accb", [128, 16 * T], F32)
    xnT = sb("xnT", [128, 16 * T], BF16)
    qt = sb("qt", [128, 16 * T], BF16)
    kt = sb("kt", [128, 16 * T], BF16)
    wring = sb("wring", [128, NS * 2048], BF16)
    tbl = sb("tblb", [128, 2 * 2048], BF16)
    maskB = sb("maskB", [128, 2048], BF16)
    sc1 = sb("sc1", [128, T], F32)
    sc2 = sb("sc2", [128, T], BF16)
    bfs = sb("bfs", [128, T], BF16)
    rows = sb("rows", [1, T], BF16)
    smallf32 = sb("smallf32b", [128, 80], F32)
    biascol = smallf32[:, 16:80]
    maskpair = sb("maskpair", [1, 1024], BF16)
    maskT = sb("maskT", [8, T], BF16)
    gatesb = sb("gatesb", [128, 64], F32)
    cntsb = sb("cntsb", [128, 64], F32)
    masksb = sb("masksb", [128, 64], BF16)
    Asb = sb("Asb", [128, 16], F32)
    A0 = sb("A0", [128, 16], F32)
    A1 = sb("A1", [128, 16], F32)
    A2a = sb("A2a", [128, 16], F32)
    A2b = sb("A2b", [128, 16], F32)
    A2 = sb("A2", [128, 16], F32)
    smallbf = sb("smallbfb", [128, 272], BF16)
    eye = smallbf[:, 0:128]
    onescol = smallbf[:, 128:129]
    onesrow = smallbf[0:1, 144:272]
    consts = smallf32[0:1, 0:8]
    bsel0 = smallf32[:, 8:9]
    bsel1 = smallf32[:, 9:10]
    wg_bf = sb("wgbf", [128, 128], BF16)

    pA = ps("pA", [128, T], F32)
    pB = ps("pB", [128, T], F32)
    pC = ps("pC", [128, T], F32)
    pD = ps("pD", [128, 512], F32)
    pT = ps("pT", [128, T], BF16)
    PS = {"pA": pA, "pB": pB, "pC": pC}

    MM = lambda out, l, r, st, sp: (lambda E: E.matmul(
        out, l, r, start=st, stop=sp, skip_group_check=True))

    slab_event = {}
    bias_event = {}
    tbl_event = {}
    slab_ld = {}
    bias_ld = {}
    tbl_ld = {}

    B.dma("sync", "s_small", smallbf[:, :], smallbf_d)
    B.dma("sync", "s_small", smallf32[:, :], smallf32_d)
    n_small = B.sem("s_small")["n"]
    for q4 in range(8):
        eng = "scalar" if q4 in (5, 7) else "gpsimd"
        B.dma(eng, f"s_xc{q4}",
              acc[:, q4 * 2 * T:(q4 + 1) * 2 * T].rearrange("p (j n) -> p j n", j=2),
              xT_d[q4 * 256:(q4 + 1) * 256, :].rearrange("(j p) n -> p j n", p=128))
    B.dma("gpsimd", "s_wg", wg_bf[:, :].rearrange("p (c e) -> p c e", c=16),
          wg_d.rearrange("(c p) e -> p c e", p=128))

    slab_plan = []
    for t in range(48):
        slab_plan.append(("A", t))
    for o in range(16):
        slab_plan.append(("A", 48 + o))
    for p in range(4):
        for h in range(16):
            slab_plan.append(("A", 64 + 16 * p + h))
        for g in range(32):
            slab_plan.append(("B2", 32 * p + g))
    NSLAB = len(slab_plan)

    def slab_slot_ap(k):
        if k < 24 or k >= 64:
            s = k % 8 if k < 24 else (k - 64) % 8
            if s < 4:
                return s, wring[:, s * 2048:(s + 1) * 2048]
            return s, qt[:, (s - 4) * 2048:(s - 3) * 2048]
        s = k % NS
        return s, wring[:, s * 2048:(s + 1) * 2048]

    def emit_slab_dma(k):
        kind, si = slab_plan[k]
        slot, dstfull = slab_slot_ap(k)
        if k < 28 or k >= 72:
            prev = slab_event.get(k - 8)
        else:
            prev = slab_event.get(k - NS)
        if 68 <= k < 72:
            prev = ("s_twout", 16)
        if prev is not None:
            B.wait("gpsimd", prev[0], prev[1])
        if kind == "A":
            src = wsA_d[si].rearrange("(c p) q -> p c q", p=128)
            dst = dstfull.rearrange("p (c q) -> p c q", c=16)
        else:
            src = wsB_d[si].rearrange("(c p) q -> p c q", p=128)
            dst = dstfull[:, 0:1024].rearrange("p (c q) -> p c q", c=8)
        slab_ld[k] = (f"s_ws{slot}", B.dma("gpsimd", f"s_ws{slot}", dst, src))

    gp_slab_k = [0]

    def pump_slabs(upto_slab=None, upto_bias=None):
        if upto_slab is None:
            upto_slab = NSLAB
        while gp_slab_k[0] < upto_slab:
            emit_slab_dma(gp_slab_k[0]); gp_slab_k[0] += 1

    pump_slabs(upto_slab=8)

    def emit_tbl(q):
        j = q % 4
        slot = q % 2
        prev = tbl_event.get(q - 2)
        if prev is not None:
            B.wait("sync", prev[0], prev[1])
        B.dma("sync", f"s_tb{slot}", tbl[:, slot * 2048: slot * 2048 + 1024],
              tbl_d[0, 128 * j:128 * (j + 1), :])
        tbl_ld[q] = (f"s_tb{slot}",
                     B.dma("sync", f"s_tb{slot}",
                           tbl[:, slot * 2048 + 1024:(slot + 1) * 2048],
                           tbl_d[1, 128 * j:128 * (j + 1), :]))
    emit_tbl(0)
    emit_tbl(1)

    B.wait("scalar", "s_small", n_small)
    B.emit("scalar", lambda E: E.activation(consts[0:1, 3:4], consts[0:1, 3:4], AF.Ln))

    sqbuf = [bfs[:, :], maskB[:, 0:1024], maskB[:, 1024:2048]]
    B.wait("tensor", "s_small", n_small)
    for j in range(16):
        B.wait("vector", f"s_xc{j // 2}", 16)
        evcp = B.emit("vector", lambda E, j=j: E.tensor_copy(
            xnT[:, j * T:(j + 1) * T], acc[:, j * T:(j + 1) * T]))
        if j > 2:
            B.wait("vector", "s_tst", j - 2)
        ev = B.inc("vector", lambda E, j=j: E.tensor_tensor(
            sqbuf[j % 3], xnT[:, j * T:(j + 1) * T], xnT[:, j * T:(j + 1) * T],
            ALU.mult), "s_vsq")
        B.wait("tensor", "s_v", evcp)
        for h in (0, 512):
            B.emit("tensor", MM(pA[0:1, h:h + 512], onescol[:, 0:1],
                                xnT[:, j * T + h: j * T + h + 512], j == 0, j == 15))
        B.wait("tensor", "s_v", ev)
        B.emit("tensor", MM(pB[0:1, 0:512], onescol[:, 0:1], sqbuf[j % 3][:, 0:512],
                            j == 0, j == 15))
        B.inc("tensor", MM(pB[0:1, 512:1024], onescol[:, 0:1], sqbuf[j % 3][:, 512:1024],
                           j == 0, j == 15), "s_tst")
    v1 = sc1[0:1, :]
    vb = sc2[0:1, :]
    B.wait("vector", "s_tst", 16)
    B.emit("vector", lambda E: E.tensor_scalar(v1, pA[0:1, :], 1.0 / D, None, ALU.mult))
    evm = B.inc("vector", lambda E: E.tensor_copy(rows[0:1, :], v1), "s_vrows")
    B.wait("tensor", "s_v", evm)
    B.emit("tensor", MM(pA[:, 0:512], onesrow[0:1, :], rows[0:1, 0:512], True, True))
    evbm = B.inc("tensor", MM(pA[:, 512:1024], onesrow[0:1, :], rows[0:1, 512:1024],
                              True, True), "s_tbc")
    B.emit("vector", lambda E: E.tensor_scalar(vb, pB[0:1, :], 1.0 / D, None, ALU.mult))
    B.emit("vector", lambda E: E.tensor_tensor(v1, v1, v1, ALU.mult))
    evv = B.inc("vector", lambda E: E.tensor_tensor(v1, vb, v1, ALU.subtract), "s_vrows")
    B.wait("scalar", "s_v", evv)
    B.emit("scalar", lambda E: E.activation(v1, v1, AF.Ln, bias=consts[0:1, 0:1]))
    evsq = B.inc("scalar", lambda E: E.activation(v1, v1, AF.Exp,
                                                  scale=consts[0:1, 4:5]), "s_scrow")
    B.wait("vector", "s_sc", evsq)
    B.wait("vector", "s_tbc", evbm)
    evr = B.inc("vector", lambda E: E.tensor_copy(rows[0:1, :], v1), "s_vrows")
    B.wait("tensor", "s_v", evr)
    B.emit("tensor", MM(pB[:, 0:512], onesrow[0:1, :], rows[0:1, 0:512], True, True))
    evbr = B.inc("tensor", MM(pB[:, 512:1024], onesrow[0:1, :], rows[0:1, 512:1024],
                              True, True), "s_tbc")
    B.wait("vector", "s_tbc", evbr)
    mBbf = bfs[:, :]
    rBbf = maskB[:, 0:1024]
    B.emit("vector", lambda E: E.tensor_copy(mBbf, pA[:, :]))
    evev = B.inc("vector", lambda E: E.tensor_copy(rBbf, pB[:, :]), "s_vlnc")
    B.release("pA", "s_v", evev)
    B.release("pB", "s_v", evev)
    evlnc_tile = []
    for j in range(16):
        xj = xnT[:, j * T:(j + 1) * T]
        B.emit("vector", lambda E, xj=xj: E.tensor_tensor(xj, xj, mBbf, ALU.subtract))
        evlnc_tile.append(B.inc("vector", lambda E, xj=xj: E.tensor_tensor(
            xj, xj, rBbf, ALU.mult), "s_vlnc"))
    evlnc = evlnc_tile[-1]
    evlnc2 = None

    def emit_pass2(j):
        nonlocal evlnc2
        B.wait("vector", "s_xc6", 16)
        B.wait("vector", "s_xc7", 16)
        aj = acc[:, j * T:(j + 1) * T]
        xj = xnT[:, j * T:(j + 1) * T]
        evlnc2 = B.inc("vector", lambda E, aj=aj, xj=xj: E.tensor_tensor(
            aj, aj, xj, ALU.add), "s_vlnc")

    def _flush_and_return():
        for o in range(16):
            B.dma("sync", "s_out", y_d[o * 128:(o + 1) * 128, :],
                  acc[:, o * T:(o + 1) * T])
        B.wait("sync", "s_out", 16 * 16)
        _materialize(B, nc, ctx)
        return nc

    if stage <= 1:
        for j in range(16):
            emit_pass2(j)
        B.wait("sync", "s_v", evlnc2)
        return _flush_and_return()

    stream = [("K", t) for t in PAIRED] + [("V", t) for t in range(16)] + \
             [("Q", t) for t in PAIRED]

    def qkv_group(idx, slotname):
        slot = PS[slotname]
        k = idx
        B.wait("tensor", *slab_ld[k])
        B.use("tensor", slotname)
        ws = slab_slot_ap(k)[1]
        ev = None
        for d in range(16):
            if idx < 2:
                B.wait("tensor", "s_v", evlnc_tile[d])
            for h in (0, 512):
                fn = MM(slot[:, h:h + 512], ws[:, d * 128:(d + 1) * 128],
                        xnT[:, d * T + h: d * T + h + 512], d == 0, d == 15)
                if d == 15 and h == 512:
                    ev = B.inc("tensor", fn, "s_tqkv")
                else:
                    B.emit("tensor", fn)
        slab_event[k] = ("s_tqkv", ev)
        pump_slabs(upto_slab=min(k + 1 + (8 if (k >= 64 or k < 20) else NS), NSLAB))
        return ev

    sc1b = sc1[:, :].bitcast(BF16)

    def rope_pair(block, j, se, so, ev_odd, dest, tblq, ke, ko):
        tslot = tblq % 2
        cosb = tbl[:, tslot * 2048: tslot * 2048 + 1024]
        sinb = tbl[:, tslot * 2048 + 1024:(tslot + 1) * 2048]
        pse, pso = PS[se], PS[so]
        dE = dest[:, j * T:(j + 1) * T]
        dO = dest[:, (4 + j) * T:(5 + j) * T]
        B.wait("scalar", "s_tqkv", ev_odd)
        ru = B.inc("scalar", lambda E, dE=dE: E.activation(
            dE, pse[:, :], AF.Identity, bias=biascol[:, ke:ke + 1]), "s_scp")
        rv = B.inc("scalar", lambda E, dO=dO: E.activation(
            dO, pso[:, :], AF.Identity, bias=biascol[:, ko:ko + 1]), "s_scp")
        B.release(se, "s_sc", ru)
        B.release(so, "s_sc", rv)
        x1 = sc1b[:, 0:1024]
        x2 = sc1b[:, 1024:2048]
        x3 = sc2[:, :]
        B.wait("vector", "s_sc", rv)
        B.wait("vector", *tbl_ld[tblq])
        B.emit("vector", lambda E, dE=dE: E.tensor_tensor(x1, dE, cosb, ALU.mult))
        B.emit("vector", lambda E, dO=dO: E.tensor_tensor(x2, dO, sinb, ALU.mult))
        B.emit("vector", lambda E, dO=dO: E.tensor_tensor(x3, dO, cosb, ALU.mult))
        B.emit("vector", lambda E, dE=dE, dO=dO: E.tensor_tensor(dO, dE, sinb, ALU.mult))
        B.emit("vector", lambda E, dE=dE: E.tensor_tensor(dE, x1, x2, ALU.subtract))
        r1 = B.inc("vector", lambda E, dO=dO: E.tensor_tensor(dO, x3, dO, ALU.add),
                   "s_vqkv")
        tbl_event[tblq] = ("s_v", r1)
        if tblq + 2 <= 7:
            emit_tbl(tblq + 2)
        return r1

    ringK = ["pC", "pA", "pB"]
    pend = {}
    kpair_done_ev = {}
    sq_emitted = [0]
    sq_ready = {}
    sq_ev = {}

    sq_pos = {}

    def emit_kk_squares(pos):
        while sq_emitted[0] in sq_ready:
            j = sq_emitted[0]
            B.wait("vector", *sq_ready[j])
            if j > 2:
                B.wait("vector", "s_tnsq", j - 2)
            sq_ev[j] = B.inc("vector", lambda E, j=j: E.tensor_tensor(
                sqbuf[j % 3], kt[:, j * T:(j + 1) * T], kt[:, j * T:(j + 1) * T],
                ALU.mult), "s_vsq")
            sq_pos[j] = pos
            sq_emitted[0] += 1

    nsq_emitted = [0]
    nsq_used_pB = [False]

    def emit_nsq(limit=16, pos=99):
        while nsq_emitted[0] < limit and nsq_emitted[0] in sq_ev \
                and sq_pos.get(nsq_emitted[0], 99) <= pos - 2:
            j = nsq_emitted[0]
            if not nsq_used_pB[0]:
                B.use("tensor", "pB")
                nsq_used_pB[0] = True
            B.wait("tensor", "s_v", sq_ev[j])
            B.emit("tensor", MM(pB[0:1, 0:512], onescol[:, 0:1], sqbuf[j % 3][:, 0:512],
                                j == 0, j == 15))
            B.inc("tensor", MM(pB[0:1, 512:1024], onescol[:, 0:1],
                               sqbuf[j % 3][:, 512:1024], j == 0, j == 15), "s_tnsq")
            nsq_emitted[0] += 1

    for i in range(16):
        blk, t = stream[i]
        slotname = ringK[i % 3] if i < 8 else ["pC", "pA"][i % 2]
        ev = qkv_group(i, slotname)
        if t < 4:
            pend[t] = (slotname, ev, i)
        elif t < 8:
            jj = t - 4
            se, ev_e, ke = pend.pop(jj)
            kpair_done_ev[jj] = rope_pair("K", jj, se, slotname, ev, kt, jj, ke, i)
            sq_ready[jj] = ("s_v", kpair_done_ev[jj])
            sq_ready[4 + jj] = ("s_v", kpair_done_ev[jj])
        else:
            B.wait("scalar", "s_tqkv", ev)
            r = B.inc("scalar", lambda E, t=t, s=PS[slotname], i=i: E.activation(
                kt[:, t * T:(t + 1) * T], s[:, :], AF.Identity,
                bias=biascol[:, i:i + 1]), "s_scp")
            B.release(slotname, "s_sc", r)
            sq_ready[t] = ("s_sc", r)
        emit_kk_squares(i)
        if i >= 6:
            emit_nsq(pos=i)

    if stage == 11:
        B.wait("sync", "s_v", B.sem("s_v")["n"])
        B.wait("sync", "s_sc", B.sem("s_sc")["n"])
        return _flush_and_return()

    emit_kk_squares(16)
    assert sq_emitted[0] == 16
    emit_nsq()
    assert nsq_emitted[0] == 16
    B.wait("scalar", "s_tnsq", 16)
    B.emit("scalar", lambda E: E.activation(sc1[0:1, :], pB[0:1, :], AF.Relu,
                                            bias=consts[0:1, 6:7]))
    B.emit("scalar", lambda E: E.activation(sc1[0:1, :], sc1[0:1, :], AF.Ln,
                                            bias=consts[0:1, 5:6]))
    evr2 = B.inc("scalar", lambda E: E.activation(rows[0:1, :], sc1[0:1, :], AF.Exp,
                                                  scale=consts[0:1, 4:5]), "s_scrow")
    B.wait("tensor", "s_sc", evr2)
    B.emit("tensor", MM(pB[:, 0:512], onesrow[0:1, :], rows[0:1, 0:512], True, True))
    evrn = B.inc("tensor", MM(pB[:, 512:1024], onesrow[0:1, :], rows[0:1, 512:1024],
                              True, True), "s_trn")

    if stage == 12:
        B.wait("sync", "s_trn", 1)
        return _flush_and_return()

    for i in range(16, 32):
        blk, t = stream[i]
        slotname = ["pC", "pA"][i % 2]
        ev = qkv_group(i, slotname)
        B.wait("vector", "s_tqkv", ev)
        B.wait("vector", "s_trn", evrn)
        slot = PS[slotname]
        B.emit("vector", lambda E, t=t: E.tensor_tensor(
            sc1[:, :], kt[:, t * T:(t + 1) * T], pB[:, :], ALU.mult))
        rp = B.inc("vector", lambda E, slot=slot, i=i: E.scalar_tensor_tensor(
            sc2[:, :], slot[:, :], biascol[:, i:i + 1], sc1[:, :], ALU.add, ALU.mult),
            "s_vqkv")
        r = B.inc("vector", lambda E, t=t: E.reduce_sum(
            Asb[:, t:t + 1], sc2[:, :], axis=AX.X), "s_vqkv")
        B.release(slotname, "s_v", rp)
        emit_pass2(i - 16)
    B.release("pB", "s_v", B.sem("s_v")["n"])

    if stage == 13:
        B.wait("sync", "s_v", B.sem("s_v")["n"])
        return _flush_and_return()

    B.emit("vector", lambda E: E.tensor_scalar(A0[:, :], Asb[:, :], bsel0[:, 0:1],
                                               None, ALU.mult))
    evA = B.inc("vector", lambda E: E.tensor_scalar(A1[:, :], Asb[:, :], bsel1[:, 0:1],
                                                    None, ALU.mult), "s_vA")
    B.wait("sync", "s_v", evA)
    B.dma("sync", "s_dda", cc_in.ap()[0:1, :].rearrange("o (p j) -> (o p) j", p=128), A0[:, :])
    B.dma("sync", "s_dda", cc_in.ap()[1:2, :].rearrange("o (p j) -> (o p) j", p=128), A1[:, :])

    def emit_collective():
        B.wait("gpsimd", "s_dda", 32)
        B.inc("gpsimd", lambda E: E.collective_compute(
            "AllReduce", ALU.add, replica_groups=[list(range(NC))],
            ins=[cc_in.ap().opt()], outs=[cc_out.ap().opt()]), "s_cc")
        B.wait("sync", "s_cc", 1)
        B.dma("sync", "s_a2",
              A2a[:, :], cc_out.ap()[0:1, :].rearrange("o (p j) -> (o p) j", p=128))
        B.dma("sync", "s_a2",
              A2b[:, :], cc_out.ap()[1:2, :].rearrange("o (p j) -> (o p) j", p=128))

    if stage == 14:
        emit_collective()
        B.wait("sync", "s_a2", 32)
        return _flush_and_return()

    pend = {}
    ringQ = ["pA", "pB", "pC"]
    ev_aqt = None
    a2_done = [False]
    aqt_q = []

    def emit_aqt(j, prod_sem, prod_val):
        nonlocal ev_aqt
        if not a2_done[0]:
            B.wait("vector", "s_a2", 32)
            B.emit("vector", lambda E: E.tensor_scalar(A2[:, :], A2a[:, :], bsel0[:, 0:1],
                                                       None, ALU.mult))
            evA2 = B.inc("vector", lambda E: E.scalar_tensor_tensor(
                A2[:, :], A2b[:, :], bsel1[:, 0:1], A2[:, :], ALU.mult, ALU.add), "s_va2c")
            B.wait("scalar", "s_v", evA2)
            a2_done[0] = True
        B.wait("scalar", prod_sem, prod_val)
        ev_aqt = B.inc("scalar", lambda E, j=j: E.activation(
            qt[:, j * T:(j + 1) * T], qt[:, j * T:(j + 1) * T], AF.Identity,
            scale=A2[:, j:j + 1]), "s_vaqt")

    def flush_aqt(pos):
        while aqt_q and aqt_q[0][3] <= pos - 2:
            j, sem, val, _ = aqt_q.pop(0)
            emit_aqt(j, sem, val)

    for i in range(32, 48):
        blk, t = stream[i]
        slotname = ringQ[(i - 32) % 3]
        ev = qkv_group(i, slotname)
        if t < 4:
            pend[t] = (slotname, ev, i)
        elif t < 8:
            jj = t - 4
            se, ev_e, ke = pend.pop(jj)
            r1 = rope_pair("Q", jj, se, slotname, ev, qt, 4 + jj, ke, i)
            aqt_q.append((jj, "s_v", r1, i))
            aqt_q.append((4 + jj, "s_v", r1, i))
        else:
            B.wait("scalar", "s_tqkv", ev)
            r = B.inc("scalar", lambda E, t=t, s=PS[slotname], i=i: E.activation(
                qt[:, t * T:(t + 1) * T], s[:, :], AF.Identity,
                bias=biascol[:, i:i + 1]), "s_scp")
            B.release(slotname, "s_sc", r)
            aqt_q.append((t, "s_sc", r, i))
        if i == 35:
            emit_collective()
        if i >= 44:
            flush_aqt(i)
    flush_aqt(99)

    if stage == 15:
        B.wait("sync", "s_v", B.sem("s_v")["n"])
        B.wait("sync", "s_sc", B.sem("s_sc")["n"])
        B.wait("sync", "s_a2", 32)
        return _flush_and_return()

    B.wait("tensor", "s_wg", 16)
    B.use("tensor", "pD")
    evg = None
    for nj in range(8):
        for d in range(16):
            fn = MM(pD[:, nj * 8:(nj + 1) * 8],
                    xnT[:, d * T + nj * 128: d * T + (nj + 1) * 128],
                    wg_bf[:, d * 8:(d + 1) * 8], d == 0, d == 15)
            if nj == 7 and d == 15:
                evg = B.inc("tensor", fn, "s_tgate")
            else:
                B.emit("tensor", fn)


    B.wait("vector", "s_tgate", evg)
    B.emit("vector", lambda E: E.tensor_copy(gatesb[:, :], pD[:, 0:64]))
    evgc = B.inc("vector", lambda E: E.memset(cntsb[:, :], 0.0), "s_vgate")
    B.release("pD", "s_v", evgc)
    V3 = lambda t, lo, w: t[:, 0:64].rearrange("p (nj e) -> p nj e", nj=8)[:, :, lo:lo + w]
    for k in range(1, 8):
        w = 8 - k
        B.emit("vector", lambda E, k=k, w=w: E.tensor_tensor(
            V3(sc1, 0, w), V3(gatesb, k, w), V3(gatesb, 0, w), ALU.is_gt))
        B.emit("vector", lambda E, k=k, w=w: E.tensor_tensor(
            V3(cntsb, 0, w), V3(cntsb, 0, w), V3(sc1, 0, w), ALU.add))
        B.emit("vector", lambda E, k=k, w=w: E.tensor_tensor(
            V3(sc1, 0, w), V3(gatesb, 0, w), V3(gatesb, k, w), ALU.is_ge))
        B.emit("vector", lambda E, k=k, w=w: E.tensor_tensor(
            V3(cntsb, k, w), V3(cntsb, k, w), V3(sc1, 0, w), ALU.add))
    evmask = B.inc("vector", lambda E: E.tensor_scalar(
        masksb[:, :], cntsb[:, :], 4.0, None, ALU.is_lt), "s_vgate")


    for o in range(16):
        k = 48 + o
        slotname = ringK[o % 3]
        slot = PS[slotname]
        B.wait("tensor", "s_sc", ev_aqt)
        B.wait("tensor", *slab_ld[k])
        B.use("tensor", slotname)
        ws = slab_slot_ap(k)[1]
        ev = None
        for d in range(16):
            for h in (0, 512):
                fn = MM(slot[:, h:h + 512], ws[:, d * 128:(d + 1) * 128],
                        qt[:, d * T + h: d * T + h + 512], d == 0, d == 15)
                if d == 15 and h == 512:
                    ev = B.inc("tensor", fn, "s_twout")
                else:
                    B.emit("tensor", fn)
        slab_event[k] = ("s_twout", ev)
        pump_slabs(upto_slab=min(k + 1 + (8 if (k >= 64 or k < 20) else NS), NSLAB))
        B.wait("vector", "s_twout", ev)
        aj = acc[:, o * T:(o + 1) * T]
        r = B.inc("vector", lambda E, aj=aj, slot=slot, k=k: E.scalar_tensor_tensor(
            aj, slot[:, :], biascol[:, k:k + 1], aj, ALU.add, ALU.add), "s_vwout")
        B.release(slotname, "s_v", r)
        ev_wout_last = r
        if o == 1:
            B.wait("tensor", "s_v", evmask)
            B.use("tensor", "pT")
            evmt = None
            for nj in range(8):
                fn = lambda E, nj=nj: E.transpose(
                    pT[0:8, nj * 128:(nj + 1) * 128], masksb[:, nj * 8:(nj + 1) * 8], eye[:, :])
                if nj < 7:
                    B.emit("tensor", fn)
                else:
                    evmt = B.inc("tensor", fn, "s_tmt")
            B.wait("vector", "s_tmt", evmt)
            evmTc = B.inc("vector", lambda E: E.tensor_copy(maskT[0:8, :], pT[0:8, :]), "s_vmaskT")
            B.release("pT", "s_v", evmTc)
            B.wait("sync", "s_v", evmTc)
            B.dma("sync", "s_ddm", mbounce.ap()[:, :], maskT[0:8, :])

    if stage <= 3:
        B.wait("sync", "s_v", ev_wout_last)
        return _flush_and_return()


    res_u = {"u0": ("pC", lambda: pC[:, 0:512]),
             "u1": ("pC", lambda: pC[:, 512:1024]),
             "u2": ("pD", lambda: pD[:, 0:512])}
    B.res["u0"] = B.res.get("pC")
    B.res["u1"] = B.res.get("pC")
    B.res["u2"] = B.res.get("pD")

    w1slot = ["pA", "pB"]
    hw_counter = [0]
    for p in range(4):
        mb_ev = [None, None]
        mb_step = [0]

        def emit_maskB_step(p=p, mb_ev=mb_ev, mb_step=mb_step):
            eL = mb_step[0]
            if eL >= 2:
                return
            B.wait("sync", "s_ddm", 16)
            if p > 0 or eL > 0:
                B.wait("sync", "s_tmb", 4 * p + 2 * eL)
            B.dma("sync", "s_mp", maskpair[0:1, :],
                  mbounce.ap()[2 * p + eL:2 * p + eL + 1, :])
            for hh in range(2):
                B.wait("tensor", "s_mp", 16 * (2 * p + eL + 1))
                B.use("tensor", "u2")
                evb = B.inc("tensor", MM(pD[:, 0:512], onesrow[0:1, :],
                                         maskpair[0:1, hh * 512: hh * 512 + 512],
                                         True, True), "s_tmb")
                B.wait("vector", "s_tmb", evb)
                r = B.inc("vector", lambda E, eL=eL, hh=hh: E.tensor_copy(
                    maskB[:, eL * 1024 + hh * 512: eL * 1024 + hh * 512 + 512],
                    pD[:, 0:512]), "s_vmb")
                B.release("u2", "s_v", r)
                mb_ev[eL] = r
            mb_step[0] += 1

        for eL in range(2):
            for h in range(8):
                hw = hw_counter[0]
                k = 64 + 48 * p + 8 * eL + h
                slotname = w1slot[hw % 2]
                slot = PS[slotname]
                B.wait("tensor", *slab_ld[k])
                B.use("tensor", slotname)
                ws = slab_slot_ap(k)[1]
                ev = None
                for hh in (0, 512):
                    for d in range(16):
                        fn = MM(slot[:, hh:hh + 512], ws[:, d * 128:(d + 1) * 128],
                                xnT[:, d * T + hh: d * T + hh + 512], d == 0, d == 15)
                        if d == 15 and hh == 512:
                            ev = B.inc("tensor", fn, "s_tw1")
                        else:
                            B.emit("tensor", fn)
                slab_event[k] = ("s_tw1", ev)
                pump_slabs(upto_slab=min(k + 1 + (8 if (k >= 64 or k < 20) else NS), NSLAB))
                if eL == 0 and h < 2:
                    emit_maskB_step()
                B.wait("vector", "s_tw1", ev)
                B.wait("vector", "s_v", mb_ev[eL])
                if p > 0:
                    B.wait("vector", "s_tw2", 32 * p)
                dst = kt[:, (eL * 8 + h) * T:(eL * 8 + h + 1) * T]
                r = B.inc("vector", lambda E, dst=dst, slot=slot, eL=eL: E.scalar_tensor_tensor(
                    dst, slot[:, :], 0.0, maskB[:, eL * 1024:(eL + 1) * 1024],
                    ALU.max, ALU.mult), "s_vhT")
                B.release(slotname, "s_v", r)
                ev_hT_last = r
                hw_counter[0] += 1
        for g in range(32):
            o, nh = g // 2, g % 2
            uname = ["u0", "u1", "u2"][g % 3]
            psname, uap = res_u[uname]
            k_e0 = 64 + 48 * p + 16 + 2 * o
            k_e1 = k_e0 + 1
            B.wait("tensor", *slab_ld[k_e0])
            B.wait("tensor", *slab_ld[k_e1])
            B.wait("tensor", "s_v", ev_hT_last)
            B.use("tensor", uname)
            first = True
            ev = None
            for eL, kk in ((0, k_e0), (1, k_e1)):
                ws = slab_slot_ap(kk)[1][:, 0:1024]
                for hc in range(8):
                    last = (eL == 1 and hc == 7)
                    fn = MM(uap()[:, :], ws[:, hc * 128:(hc + 1) * 128],
                            kt[:, (eL * 8 + hc) * T + nh * 512:
                               (eL * 8 + hc) * T + nh * 512 + 512],
                            first, last)
                    if last:
                        ev = B.inc("tensor", fn, "s_tw2")
                    else:
                        B.emit("tensor", fn)
                    first = False
            if nh == 1:
                slab_event[k_e0] = ("s_tw2", ev)
                slab_event[k_e1] = ("s_tw2", ev)
                pump_slabs(upto_slab=min(k_e1 + 1 + 8, NSLAB))
            B.wait("vector", "s_tw2", ev)
            aslice = acc[:, o * T + nh * 512: o * T + nh * 512 + 512]
            r = B.inc("vector", lambda E, aslice=aslice, uap=uap: E.tensor_tensor(
                aslice, aslice, uap()[:, :], ALU.add), "s_vw2")
            B.release(uname, "s_v", r)
            if p == 3 and nh == 1:
                B.wait("sync", "s_v", B.sem("s_v")["n"])
                B.dma("sync", "s_out", y_d[o * 128:(o + 1) * 128, :],
                      acc[:, o * T:(o + 1) * T])

    B.wait("sync", "s_out", 16 * 16)
    _materialize(B, nc, ctx)
    return nc


def _materialize(B, nc, ctx):
    from contextlib import ExitStack
    with ExitStack() as outer:
        for name, s in B.sems.items():
            s["h"] = outer.enter_context(nc.semaphore(name))
        block = outer.enter_context(nc.Block())

        @block.sync
        def _(E):
            for t in B.th["sync"]:
                t(E)

        @block.gpsimd
        def _(E):
            for t in B.th["gpsimd"]:
                t(E)

        @block.tensor
        def _(E):
            for t in B.th["tensor"]:
                t(E)

        @block.vector
        def _(E):
            for t in B.th["vector"]:
                t(E)

        @block.scalar
        def _(E):
            for t in B.th["scalar"]:
                t(E)


_NC_CACHE = []


def _get_program():
    if not _NC_CACHE:
        _NC_CACHE.append(build_program())
    return _NC_CACHE[0]


def _host_prep(L0_attn_X, L0_attn_W_in, L0_attn_b_in, L0_attn_W_out, L0_attn_b_out,
               L0_ffn_wg, L0_ffn_w1, L0_ffn_w2):
    f32 = np.float32
    bf = ml_dtypes.bfloat16
    sigma = np.concatenate([np.arange(0, 1024, 2), np.arange(1, 1024, 2),
                            np.arange(1024, 2048)])
    rows_order = np.concatenate([2048 + sigma, 4096 + sigma, sigma])

    Wnew = np.asarray(L0_attn_W_in, f32)[rows_order]
    bnew = np.asarray(L0_attn_b_in, f32)[rows_order]
    Wop = np.asarray(L0_attn_W_out, f32)[:, :]
    Wop = Wop[:, sigma]
    bout = np.asarray(L0_attn_b_out, f32)
    wg = np.ascontiguousarray(np.asarray(L0_ffn_wg, f32))
    w1 = np.asarray(L0_ffn_w1, f32)
    w2 = np.asarray(L0_ffn_w2, f32)

    stream_tiles = PAIRED + [16 + t for t in range(16)] + [32 + t for t in PAIRED]
    slabsA = np.empty((128, D, 128), f32)
    biasstream = np.empty((64, 128), f32)
    for i, t in enumerate(stream_tiles):
        slabsA[i] = Wnew[128 * t:128 * (t + 1), :].T
        biasstream[i] = bnew[128 * t:128 * (t + 1)]
    for o in range(16):
        slabsA[48 + o] = Wop[128 * o:128 * (o + 1), :].T
        biasstream[48 + o] = bout[128 * o:128 * (o + 1)]
    for p in range(4):
        for eL in range(2):
            e = 2 * p + eL
            for h in range(8):
                slabsA[64 + 16 * p + 8 * eL + h] = w1[e, 128 * h:128 * (h + 1), :].T
    slabsB = np.empty((128, 1024, 128), f32)
    for p in range(4):
        for o in range(16):
            for eL in range(2):
                e = 2 * p + eL
                slabsB[32 * p + 2 * o + eL] = w2[e, 128 * o:128 * (o + 1), :].T

    X = np.asarray(L0_attn_X, f32)
    inv = (10000.0 ** (-(np.arange(0, 1024, 2, dtype=np.float64)) / 1024.0))
    in_maps = []
    smallbf = np.zeros((128, 272), bf)
    smallbf[:, 0:128] = np.eye(128, dtype=bf)
    smallbf[:, 128] = bf(1)
    smallbf[0, 144:272] = bf(1)
    common = dict(wsA=slabsA, wsB=slabsB, wg=wg, smallbf=smallbf)
    for c in range(NC):
        b, q = c // 4, c % 4
        pos = np.arange(q * 1024, (q + 1) * 1024, dtype=np.float64)
        ang = pos[None, :] * inv[:, None]
        tblc = np.empty((2, 512, T), bf)
        tblc[0] = np.cos(ang).astype(np.float32).astype(bf)
        tblc[1] = np.sin(ang).astype(np.float32).astype(bf)
        m = dict(common)
        m["xT"] = np.ascontiguousarray(X[b, q * 1024:(q + 1) * 1024, :].T)
        m["tbl"] = tblc
        sf = np.zeros((128, 80), f32)
        sf[0, 0:8] = [EPS, 1e-12, 0.0, 1.0, -0.5, 1e-24, -1e-24, 0]
        sf[:, 8] = 1.0 if b == 0 else 0.0
        sf[:, 9] = 1.0 if b == 1 else 0.0
        sf[:, 16:80] = biasstream.T
        m["smallf32"] = sf
        in_maps.append(m)
    return in_maps


def kernel(**inputs):
    in_maps = _host_prep(**inputs)
    nc = _get_program()
    res = run_bass_kernel_spmd(nc, in_maps, core_ids=list(range(NC)))
    out = np.empty((2, 4096, D), np.float32)
    for c in range(NC):
        b, q = c // 4, c % 4
        out[b, q * 1024:(q + 1) * 1024, :] = res.results[c]["y"].T
    return out


if __name__ == "__main__":
    import reference as _ref
    inp = {k: np.asarray(v) for k, v in _ref.setup_inputs().items()}
    got = kernel(**inp)
    want = np.asarray(_ref.reference(**inp))
    num = float(((got - want) ** 2).mean())
    den = float((want ** 2).mean())
    print(f"resid_var_ratio={num / den:.3e} max_abs={np.max(np.abs(got - want)):.3e}")


# version 32
# speedup vs baseline: 1.0072x; 1.0044x over previous
import sys
sys.path.insert(0, '/opt/trn_rl_repo')
import numpy as np
import ml_dtypes
import concourse.bass as bass
import concourse.mybir as mybir
from concourse.bass_utils import run_bass_kernel_spmd

F32 = mybir.dt.float32
BF16 = mybir.dt.bfloat16
AF = mybir.ActivationFunctionType
ALU = mybir.AluOpType
AX = mybir.AxisListType

D = 2048
T = 1024
NC = 8
NS = 4
EPS = 1e-5

PAIRED = [0, 4, 8, 1, 5, 9, 2, 6, 10, 3, 7, 11, 12, 13, 14, 15]


class Builder:
    def __init__(self):
        self.nc = bass.Bass("TRN2", target_bir_lowering=False)
        self.th = {k: [] for k in ("sync", "gpsimd", "tensor", "vector", "scalar")}
        self.sems = {}
        self.last_wait = {k: {} for k in self.th}
        self.res = {}

    def sem(self, name):
        if name not in self.sems:
            self.sems[name] = dict(n=0, h=None)
        return self.sems[name]

    def wait(self, eng, semname, val):
        if val <= 0:
            return
        if self.last_wait[eng].get(semname, 0) >= val:
            return
        self.last_wait[eng][semname] = val
        s = self.sem(semname)
        self.th[eng].append(lambda E, s=s, v=val: E.wait_ge(s["h"], v))

    ESEM = {"vector": "s_v", "scalar": "s_sc"}

    def emit(self, eng, fn):
        if eng in self.ESEM:
            return self.inc(eng, fn, None)
        self.th[eng].append(fn)
        return None

    def inc(self, eng, fn, semname):
        if eng in self.ESEM:
            semname = self.ESEM[eng]
            s = self.sem(semname)
            if s["n"] >= 1:
                self.wait(eng, semname, s["n"])
        else:
            s = self.sem(semname)
        s["n"] += 1
        v = s["n"]
        self.th[eng].append(lambda E, f=fn, s=s: f(E).then_inc(s["h"], 1))
        return v

    def nop_inc(self, eng, semname):
        s = self.sem(semname)
        s["n"] += 1
        v = s["n"]
        if eng == "tensor":
            self.th[eng].append(lambda E, s=s: E.nop().then_inc(s["h"], 1))
        else:
            self.th[eng].append(lambda E, s=s: E.engine_nop().then_inc(s["h"], 1))
        return v

    def dma(self, eng, semname, out_ap, in_ap):
        s = self.sem(semname)
        s["n"] += 16
        v = s["n"]
        self.th[eng].append(
            lambda E, o=out_ap, i=in_ap, s=s: E.dma_start(o, i).then_inc(s["h"], 16))
        return v

    def use(self, eng, resname):
        r = self.res.get(resname)
        if r is not None:
            self.wait(eng, r[0], r[1])

    def release(self, resname, semname, val):
        self.res[resname] = (semname, val)


def build_program(stage=4):
    B = Builder()
    nc = B.nc
    dt = nc.dram_tensor

    xT_d = dt("xT", [D, T], F32, kind="ExternalInput").ap()
    wsA_d = dt("wsA", [128, D, 128], F32, kind="ExternalInput").ap()
    wsB_d = dt("wsB", [128, 1024, 128], F32, kind="ExternalInput").ap()
    bstr_d = dt("bstr", [128, 64], F32, kind="ExternalInput").ap()
    tbl_d = dt("tbl", [2, 512, T], BF16, kind="ExternalInput").ap()
    wg_d = dt("wg", [D, 8], F32, kind="ExternalInput").ap()
    eye_d = dt("eye", [128, 128], BF16, kind="ExternalInput").ap()
    onescol_d = dt("onescol", [128, 1], BF16, kind="ExternalInput").ap()
    onesrow_d = dt("onesrow", [1, 128], BF16, kind="ExternalInput").ap()
    onesrhs_d = dt("onesrhs", [1, 512], BF16, kind="ExternalInput").ap()
    consts_d = dt("consts", [1, 8], F32, kind="ExternalInput").ap()
    bsel0_d = dt("bsel0", [128, 1], F32, kind="ExternalInput").ap()
    bsel1_d = dt("bsel1", [128, 1], F32, kind="ExternalInput").ap()
    y_d = dt("y", [D, T], F32, kind="ExternalOutput").ap()
    cc_in = dt("cc_in", [2, D], F32, kind="Internal")
    cc_out = dt("cc_out", [2, D], F32, kind="Internal")
    mbounce = dt("mbounce", [8, T], BF16, kind="Internal")

    from contextlib import ExitStack
    ctx = ExitStack()
    sb = lambda name, shape, dtp: ctx.enter_context(nc.sbuf_tensor(name, shape, dtp))
    ps = lambda name, shape, dtp: ctx.enter_context(nc.psum_tensor(name, shape, dtp))

    acc = sb("acc", [128, 16 * T], F32)
    xnT = sb("xnT", [128, 16 * T], BF16)
    qt = sb("qt", [128, 16 * T], BF16)
    kt = sb("kt", [128, 16 * T], BF16)
    wring = sb("wring", [128, NS * 2048], BF16)
    tbl = sb("tblb", [128, 2 * 2048], BF16)
    maskB = sb("maskB", [128, 2048], BF16)
    sc1 = sb("sc1", [128, T], F32)
    sc2 = sb("sc2", [128, T], BF16)
    bfs = sb("bfs", [128, T], BF16)
    rows = sb("rows", [1, T], BF16)
    biascol = sb("biascol", [128, 64], F32)
    maskpair = sb("maskpair", [1, 1024], BF16)
    maskT = sb("maskT", [8, T], BF16)
    gatesb = sb("gatesb", [128, 64], F32)
    cntsb = sb("cntsb", [128, 64], F32)
    masksb = sb("masksb", [128, 64], BF16)
    Asb = sb("Asb", [128, 16], F32)
    A0 = sb("A0", [128, 16], F32)
    A1 = sb("A1", [128, 16], F32)
    A2a = sb("A2a", [128, 16], F32)
    A2b = sb("A2b", [128, 16], F32)
    A2 = sb("A2", [128, 16], F32)
    eye = sb("eyeb", [128, 128], BF16)
    onescol = sb("onescolb", [128, 1], BF16)
    onesrow = sb("onesrowb", [1, 128], BF16)
    onesrhs = sb("onesrhsb", [1, 512], BF16)
    consts = sb("constsb", [1, 8], F32)
    bsel0 = sb("bsel0b", [128, 1], F32)
    bsel1 = sb("bsel1b", [128, 1], F32)
    wg_bf = sb("wgbf", [128, 128], BF16)

    pA = ps("pA", [128, T], F32)
    pB = ps("pB", [128, T], F32)
    pC = ps("pC", [128, T], F32)
    pD = ps("pD", [128, 512], F32)
    pT = ps("pT", [128, T], BF16)
    PS = {"pA": pA, "pB": pB, "pC": pC}

    MM = lambda out, l, r, st, sp: (lambda E: E.matmul(
        out, l, r, start=st, stop=sp, skip_group_check=True))

    slab_event = {}
    bias_event = {}
    tbl_event = {}
    slab_ld = {}
    bias_ld = {}
    tbl_ld = {}

    for nm, dst, src in (("eye", eye, eye_d), ("oc", onescol, onescol_d),
                         ("or", onesrow, onesrow_d), ("orh", onesrhs, onesrhs_d),
                         ("cn", consts, consts_d), ("b0", bsel0, bsel0_d),
                         ("b1", bsel1, bsel1_d)):
        B.dma("sync", "s_small", dst[:, :], src)
    B.dma("sync", "s_small", biascol[:, :], bstr_d)
    n_small = B.sem("s_small")["n"]
    emit_tbl_early = True
    for q4 in range(8):
        B.dma("gpsimd", f"s_xbf{q4}",
              xnT[:, q4 * 2 * T:(q4 + 1) * 2 * T].rearrange("p (j n) -> p j n", j=2),
              xT_d[q4 * 256:(q4 + 1) * 256, :].rearrange("(j p) n -> p j n", p=128))

    slab_plan = []
    for t in range(48):
        slab_plan.append(("A", t))
    for o in range(16):
        slab_plan.append(("A", 48 + o))
    for p in range(4):
        for h in range(16):
            slab_plan.append(("A", 64 + 16 * p + h))
        for g in range(32):
            slab_plan.append(("B2", 32 * p + g))
    NSLAB = len(slab_plan)

    def slab_slot_ap(k):
        if k < 24 or k >= 64:
            s = k % 8 if k < 24 else (k - 64) % 8
            if s < 4:
                return s, wring[:, s * 2048:(s + 1) * 2048]
            return s, qt[:, (s - 4) * 2048:(s - 3) * 2048]
        s = k % NS
        return s, wring[:, s * 2048:(s + 1) * 2048]

    def emit_slab_dma(k):
        kind, si = slab_plan[k]
        slot, dstfull = slab_slot_ap(k)
        if k < 28 or k >= 72:
            prev = slab_event.get(k - 8)
        else:
            prev = slab_event.get(k - NS)
        if 68 <= k < 72:
            prev = ("s_twout", 16)
        if prev is not None:
            B.wait("gpsimd", prev[0], prev[1])
        if kind == "A":
            src = wsA_d[si].rearrange("(c p) q -> p c q", p=128)
            dst = dstfull.rearrange("p (c q) -> p c q", c=16)
        else:
            src = wsB_d[si].rearrange("(c p) q -> p c q", p=128)
            dst = dstfull[:, 0:1024].rearrange("p (c q) -> p c q", c=8)
        slab_ld[k] = (f"s_ws{slot}", B.dma("gpsimd", f"s_ws{slot}", dst, src))

    gp_slab_k = [0]

    def pump_slabs(upto_slab=None, upto_bias=None):
        if upto_slab is None:
            upto_slab = NSLAB
        while gp_slab_k[0] < upto_slab:
            emit_slab_dma(gp_slab_k[0]); gp_slab_k[0] += 1

    pump_slabs(upto_slab=8)
    B.dma("gpsimd", "s_wg", wg_bf[:, :].rearrange("p (c e) -> p c e", c=16),
          wg_d.rearrange("(c p) e -> p c e", p=128))

    def emit_tbl(q):
        j = q % 4
        slot = q % 2
        prev = tbl_event.get(q - 2)
        if prev is not None:
            B.wait("sync", prev[0], prev[1])
        B.dma("sync", f"s_tb{slot}", tbl[:, slot * 2048: slot * 2048 + 1024],
              tbl_d[0, 128 * j:128 * (j + 1), :])
        tbl_ld[q] = (f"s_tb{slot}",
                     B.dma("sync", f"s_tb{slot}",
                           tbl[:, slot * 2048 + 1024:(slot + 1) * 2048],
                           tbl_d[1, 128 * j:128 * (j + 1), :]))
    emit_tbl(0)
    emit_tbl(1)
    B.wait("sync", "s_xbf7", 16)
    B.dma("sync", "s_x", acc[:, :].rearrange("p (j n) -> p j n", j=16),
          xT_d.rearrange("(j p) n -> p j n", p=128))

    B.wait("scalar", "s_small", n_small)
    B.emit("scalar", lambda E: E.activation(consts[0:1, 3:4], consts[0:1, 3:4], AF.Ln))

    sqbuf = [bfs[:, :], maskB[:, 0:1024], maskB[:, 1024:2048]]
    B.wait("tensor", "s_small", n_small)
    for j in range(16):
        B.wait("vector", f"s_xbf{j // 2}", 16)
        if j > 2:
            B.wait("vector", "s_tst", j - 2)
        ev = B.inc("vector", lambda E, j=j: E.tensor_tensor(
            sqbuf[j % 3], xnT[:, j * T:(j + 1) * T], xnT[:, j * T:(j + 1) * T],
            ALU.mult), "s_vsq")
        B.wait("tensor", f"s_xbf{j // 2}", 16)
        for h in (0, 512):
            B.emit("tensor", MM(pA[0:1, h:h + 512], onescol[:, 0:1],
                                xnT[:, j * T + h: j * T + h + 512], j == 0, j == 15))
        B.wait("tensor", "s_v", ev)
        B.emit("tensor", MM(pB[0:1, 0:512], onescol[:, 0:1], sqbuf[j % 3][:, 0:512],
                            j == 0, j == 15))
        B.inc("tensor", MM(pB[0:1, 512:1024], onescol[:, 0:1], sqbuf[j % 3][:, 512:1024],
                           j == 0, j == 15), "s_tst")
    v1 = sc1[0:1, :]
    vb = sc2[0:1, :]
    B.wait("vector", "s_tst", 16)
    B.emit("vector", lambda E: E.tensor_scalar(v1, pA[0:1, :], 1.0 / D, None, ALU.mult))
    evm = B.inc("vector", lambda E: E.tensor_copy(rows[0:1, :], v1), "s_vrows")
    B.wait("tensor", "s_v", evm)
    B.emit("tensor", MM(pA[:, 0:512], onesrow[0:1, :], rows[0:1, 0:512], True, True))
    evbm = B.inc("tensor", MM(pA[:, 512:1024], onesrow[0:1, :], rows[0:1, 512:1024],
                              True, True), "s_tbc")
    B.emit("vector", lambda E: E.tensor_scalar(vb, pB[0:1, :], 1.0 / D, None, ALU.mult))
    B.emit("vector", lambda E: E.tensor_tensor(v1, v1, v1, ALU.mult))
    evv = B.inc("vector", lambda E: E.tensor_tensor(v1, vb, v1, ALU.subtract), "s_vrows")
    B.wait("scalar", "s_v", evv)
    B.emit("scalar", lambda E: E.activation(v1, v1, AF.Ln, bias=consts[0:1, 0:1]))
    evsq = B.inc("scalar", lambda E: E.activation(v1, v1, AF.Exp,
                                                  scale=consts[0:1, 4:5]), "s_scrow")
    B.wait("vector", "s_sc", evsq)
    B.wait("vector", "s_tbc", evbm)
    evr = B.inc("vector", lambda E: E.tensor_copy(rows[0:1, :], v1), "s_vrows")
    B.wait("tensor", "s_v", evr)
    B.emit("tensor", MM(pB[:, 0:512], onesrow[0:1, :], rows[0:1, 0:512], True, True))
    evbr = B.inc("tensor", MM(pB[:, 512:1024], onesrow[0:1, :], rows[0:1, 512:1024],
                              True, True), "s_tbc")
    B.wait("vector", "s_tbc", evbr)
    mBbf = bfs[:, :]
    rBbf = maskB[:, 0:1024]
    B.emit("vector", lambda E: E.tensor_copy(mBbf, pA[:, :]))
    evev = B.inc("vector", lambda E: E.tensor_copy(rBbf, pB[:, :]), "s_vlnc")
    B.release("pA", "s_v", evev)
    B.release("pB", "s_v", evev)
    evlnc_tile = []
    for j in range(16):
        xj = xnT[:, j * T:(j + 1) * T]
        B.emit("vector", lambda E, xj=xj: E.tensor_tensor(xj, xj, mBbf, ALU.subtract))
        evlnc_tile.append(B.inc("vector", lambda E, xj=xj: E.tensor_tensor(
            xj, xj, rBbf, ALU.mult), "s_vlnc"))
    evlnc = evlnc_tile[-1]
    evlnc2 = None

    def emit_pass2(j):
        nonlocal evlnc2
        B.wait("vector", "s_x", 16)
        aj = acc[:, j * T:(j + 1) * T]
        xj = xnT[:, j * T:(j + 1) * T]
        evlnc2 = B.inc("vector", lambda E, aj=aj, xj=xj: E.tensor_tensor(
            aj, aj, xj, ALU.add), "s_vlnc")

    def _flush_and_return():
        for o in range(16):
            B.dma("sync", "s_out", y_d[o * 128:(o + 1) * 128, :],
                  acc[:, o * T:(o + 1) * T])
        B.wait("sync", "s_out", 16 * 16)
        _materialize(B, nc, ctx)
        return nc

    if stage <= 1:
        for j in range(16):
            emit_pass2(j)
        B.wait("sync", "s_v", evlnc2)
        return _flush_and_return()

    stream = [("K", t) for t in PAIRED] + [("V", t) for t in range(16)] + \
             [("Q", t) for t in PAIRED]

    def qkv_group(idx, slotname):
        slot = PS[slotname]
        k = idx
        B.wait("tensor", *slab_ld[k])
        B.use("tensor", slotname)
        ws = slab_slot_ap(k)[1]
        ev = None
        for d in range(16):
            if idx < 2:
                B.wait("tensor", "s_v", evlnc_tile[d])
            for h in (0, 512):
                fn = MM(slot[:, h:h + 512], ws[:, d * 128:(d + 1) * 128],
                        xnT[:, d * T + h: d * T + h + 512], d == 0, d == 15)
                if d == 15 and h == 512:
                    ev = B.inc("tensor", fn, "s_tqkv")
                else:
                    B.emit("tensor", fn)
        slab_event[k] = ("s_tqkv", ev)
        pump_slabs(upto_slab=min(k + 1 + (8 if (k >= 64 or k < 20) else NS), NSLAB))
        return ev

    sc1b = sc1[:, :].bitcast(BF16)

    def rope_pair(block, j, se, so, ev_odd, dest, tblq, ke, ko):
        tslot = tblq % 2
        cosb = tbl[:, tslot * 2048: tslot * 2048 + 1024]
        sinb = tbl[:, tslot * 2048 + 1024:(tslot + 1) * 2048]
        pse, pso = PS[se], PS[so]
        dE = dest[:, j * T:(j + 1) * T]
        dO = dest[:, (4 + j) * T:(5 + j) * T]
        B.wait("scalar", "s_tqkv", ev_odd)
        ru = B.inc("scalar", lambda E, dE=dE: E.activation(
            dE, pse[:, :], AF.Identity, bias=biascol[:, ke:ke + 1]), "s_scp")
        rv = B.inc("scalar", lambda E, dO=dO: E.activation(
            dO, pso[:, :], AF.Identity, bias=biascol[:, ko:ko + 1]), "s_scp")
        B.release(se, "s_sc", ru)
        B.release(so, "s_sc", rv)
        x1 = sc1b[:, 0:1024]
        x2 = sc1b[:, 1024:2048]
        x3 = sc2[:, :]
        B.wait("vector", "s_sc", rv)
        B.wait("vector", *tbl_ld[tblq])
        B.emit("vector", lambda E, dE=dE: E.tensor_tensor(x1, dE, cosb, ALU.mult))
        B.emit("vector", lambda E, dO=dO: E.tensor_tensor(x2, dO, sinb, ALU.mult))
        B.emit("vector", lambda E, dO=dO: E.tensor_tensor(x3, dO, cosb, ALU.mult))
        B.emit("vector", lambda E, dE=dE, dO=dO: E.tensor_tensor(dO, dE, sinb, ALU.mult))
        B.emit("vector", lambda E, dE=dE: E.tensor_tensor(dE, x1, x2, ALU.subtract))
        r1 = B.inc("vector", lambda E, dO=dO: E.tensor_tensor(dO, x3, dO, ALU.add),
                   "s_vqkv")
        tbl_event[tblq] = ("s_v", r1)
        if tblq + 2 <= 7:
            emit_tbl(tblq + 2)
        return r1

    ringK = ["pC", "pA", "pB"]
    pend = {}
    kpair_done_ev = {}
    sq_emitted = [0]
    sq_ready = {}
    sq_ev = {}

    sq_pos = {}

    def emit_kk_squares(pos):
        while sq_emitted[0] in sq_ready:
            j = sq_emitted[0]
            B.wait("vector", *sq_ready[j])
            if j > 2:
                B.wait("vector", "s_tnsq", j - 2)
            sq_ev[j] = B.inc("vector", lambda E, j=j: E.tensor_tensor(
                sqbuf[j % 3], kt[:, j * T:(j + 1) * T], kt[:, j * T:(j + 1) * T],
                ALU.mult), "s_vsq")
            sq_pos[j] = pos
            sq_emitted[0] += 1

    nsq_emitted = [0]
    nsq_used_pB = [False]

    def emit_nsq(limit=16, pos=99):
        while nsq_emitted[0] < limit and nsq_emitted[0] in sq_ev \
                and sq_pos.get(nsq_emitted[0], 99) <= pos - 2:
            j = nsq_emitted[0]
            if not nsq_used_pB[0]:
                B.use("tensor", "pB")
                nsq_used_pB[0] = True
            B.wait("tensor", "s_v", sq_ev[j])
            B.emit("tensor", MM(pB[0:1, 0:512], onescol[:, 0:1], sqbuf[j % 3][:, 0:512],
                                j == 0, j == 15))
            B.inc("tensor", MM(pB[0:1, 512:1024], onescol[:, 0:1],
                               sqbuf[j % 3][:, 512:1024], j == 0, j == 15), "s_tnsq")
            nsq_emitted[0] += 1

    for i in range(16):
        blk, t = stream[i]
        slotname = ringK[i % 3] if i < 8 else ["pC", "pA"][i % 2]
        ev = qkv_group(i, slotname)
        if t < 4:
            pend[t] = (slotname, ev, i)
        elif t < 8:
            jj = t - 4
            se, ev_e, ke = pend.pop(jj)
            kpair_done_ev[jj] = rope_pair("K", jj, se, slotname, ev, kt, jj, ke, i)
            sq_ready[jj] = ("s_v", kpair_done_ev[jj])
            sq_ready[4 + jj] = ("s_v", kpair_done_ev[jj])
        else:
            B.wait("scalar", "s_tqkv", ev)
            r = B.inc("scalar", lambda E, t=t, s=PS[slotname], i=i: E.activation(
                kt[:, t * T:(t + 1) * T], s[:, :], AF.Identity,
                bias=biascol[:, i:i + 1]), "s_scp")
            B.release(slotname, "s_sc", r)
            sq_ready[t] = ("s_sc", r)
        emit_kk_squares(i)
        if i >= 6:
            emit_nsq(pos=i)

    if stage == 11:
        B.wait("sync", "s_v", B.sem("s_v")["n"])
        B.wait("sync", "s_sc", B.sem("s_sc")["n"])
        return _flush_and_return()

    emit_kk_squares(16)
    assert sq_emitted[0] == 16
    emit_nsq()
    assert nsq_emitted[0] == 16
    B.wait("scalar", "s_tnsq", 16)
    B.emit("scalar", lambda E: E.activation(sc1[0:1, :], pB[0:1, :], AF.Relu,
                                            bias=consts[0:1, 6:7]))
    B.emit("scalar", lambda E: E.activation(sc1[0:1, :], sc1[0:1, :], AF.Ln,
                                            bias=consts[0:1, 5:6]))
    evr2 = B.inc("scalar", lambda E: E.activation(rows[0:1, :], sc1[0:1, :], AF.Exp,
                                                  scale=consts[0:1, 4:5]), "s_scrow")
    B.wait("tensor", "s_sc", evr2)
    B.emit("tensor", MM(pB[:, 0:512], onesrow[0:1, :], rows[0:1, 0:512], True, True))
    evrn = B.inc("tensor", MM(pB[:, 512:1024], onesrow[0:1, :], rows[0:1, 512:1024],
                              True, True), "s_trn")

    if stage == 12:
        B.wait("sync", "s_trn", 1)
        return _flush_and_return()

    for i in range(16, 32):
        blk, t = stream[i]
        slotname = ["pC", "pA"][i % 2]
        ev = qkv_group(i, slotname)
        B.wait("vector", "s_tqkv", ev)
        B.wait("vector", "s_trn", evrn)
        slot = PS[slotname]
        B.emit("vector", lambda E, t=t: E.tensor_tensor(
            sc1[:, :], kt[:, t * T:(t + 1) * T], pB[:, :], ALU.mult))
        rp = B.inc("vector", lambda E, slot=slot, i=i: E.scalar_tensor_tensor(
            sc2[:, :], slot[:, :], biascol[:, i:i + 1], sc1[:, :], ALU.add, ALU.mult),
            "s_vqkv")
        r = B.inc("vector", lambda E, t=t: E.reduce_sum(
            Asb[:, t:t + 1], sc2[:, :], axis=AX.X), "s_vqkv")
        B.release(slotname, "s_v", rp)
        emit_pass2(i - 16)
    B.release("pB", "s_v", B.sem("s_v")["n"])

    if stage == 13:
        B.wait("sync", "s_v", B.sem("s_v")["n"])
        return _flush_and_return()

    B.emit("vector", lambda E: E.tensor_scalar(A0[:, :], Asb[:, :], bsel0[:, 0:1],
                                               None, ALU.mult))
    evA = B.inc("vector", lambda E: E.tensor_scalar(A1[:, :], Asb[:, :], bsel1[:, 0:1],
                                                    None, ALU.mult), "s_vA")
    B.wait("sync", "s_v", evA)
    B.dma("sync", "s_dda", cc_in.ap()[0:1, :].rearrange("o (p j) -> (o p) j", p=128), A0[:, :])
    B.dma("sync", "s_dda", cc_in.ap()[1:2, :].rearrange("o (p j) -> (o p) j", p=128), A1[:, :])

    def emit_collective():
        B.wait("gpsimd", "s_dda", 32)
        B.inc("gpsimd", lambda E: E.collective_compute(
            "AllReduce", ALU.add, replica_groups=[list(range(NC))],
            ins=[cc_in.ap().opt()], outs=[cc_out.ap().opt()]), "s_cc")
        B.wait("sync", "s_cc", 1)
        B.dma("sync", "s_a2",
              A2a[:, :], cc_out.ap()[0:1, :].rearrange("o (p j) -> (o p) j", p=128))
        B.dma("sync", "s_a2",
              A2b[:, :], cc_out.ap()[1:2, :].rearrange("o (p j) -> (o p) j", p=128))

    if stage == 14:
        emit_collective()
        B.wait("sync", "s_a2", 32)
        return _flush_and_return()

    pend = {}
    ringQ = ["pA", "pB", "pC"]
    ev_aqt = None
    a2_done = [False]
    aqt_q = []

    def emit_aqt(j, prod_sem, prod_val):
        nonlocal ev_aqt
        if not a2_done[0]:
            B.wait("vector", "s_a2", 32)
            B.emit("vector", lambda E: E.tensor_scalar(A2[:, :], A2a[:, :], bsel0[:, 0:1],
                                                       None, ALU.mult))
            evA2 = B.inc("vector", lambda E: E.scalar_tensor_tensor(
                A2[:, :], A2b[:, :], bsel1[:, 0:1], A2[:, :], ALU.mult, ALU.add), "s_va2c")
            B.wait("scalar", "s_v", evA2)
            a2_done[0] = True
        B.wait("scalar", prod_sem, prod_val)
        ev_aqt = B.inc("scalar", lambda E, j=j: E.activation(
            qt[:, j * T:(j + 1) * T], qt[:, j * T:(j + 1) * T], AF.Identity,
            scale=A2[:, j:j + 1]), "s_vaqt")

    def flush_aqt(pos):
        while aqt_q and aqt_q[0][3] <= pos - 2:
            j, sem, val, _ = aqt_q.pop(0)
            emit_aqt(j, sem, val)

    for i in range(32, 48):
        blk, t = stream[i]
        slotname = ringQ[(i - 32) % 3]
        ev = qkv_group(i, slotname)
        if t < 4:
            pend[t] = (slotname, ev, i)
        elif t < 8:
            jj = t - 4
            se, ev_e, ke = pend.pop(jj)
            r1 = rope_pair("Q", jj, se, slotname, ev, qt, 4 + jj, ke, i)
            aqt_q.append((jj, "s_v", r1, i))
            aqt_q.append((4 + jj, "s_v", r1, i))
        else:
            B.wait("scalar", "s_tqkv", ev)
            r = B.inc("scalar", lambda E, t=t, s=PS[slotname], i=i: E.activation(
                qt[:, t * T:(t + 1) * T], s[:, :], AF.Identity,
                bias=biascol[:, i:i + 1]), "s_scp")
            B.release(slotname, "s_sc", r)
            aqt_q.append((t, "s_sc", r, i))
        if i == 35:
            emit_collective()
        if i >= 44:
            flush_aqt(i)
    flush_aqt(99)

    if stage == 15:
        B.wait("sync", "s_v", B.sem("s_v")["n"])
        B.wait("sync", "s_sc", B.sem("s_sc")["n"])
        B.wait("sync", "s_a2", 32)
        return _flush_and_return()

    B.wait("tensor", "s_wg", 16)
    B.use("tensor", "pD")
    evg = None
    for nj in range(8):
        for d in range(16):
            fn = MM(pD[:, nj * 8:(nj + 1) * 8],
                    xnT[:, d * T + nj * 128: d * T + (nj + 1) * 128],
                    wg_bf[:, d * 8:(d + 1) * 8], d == 0, d == 15)
            if nj == 7 and d == 15:
                evg = B.inc("tensor", fn, "s_tgate")
            else:
                B.emit("tensor", fn)


    B.wait("vector", "s_tgate", evg)
    B.emit("vector", lambda E: E.tensor_copy(gatesb[:, :], pD[:, 0:64]))
    evgc = B.inc("vector", lambda E: E.memset(cntsb[:, :], 0.0), "s_vgate")
    B.release("pD", "s_v", evgc)
    V3 = lambda t, lo, w: t[:, 0:64].rearrange("p (nj e) -> p nj e", nj=8)[:, :, lo:lo + w]
    for k in range(1, 8):
        w = 8 - k
        B.emit("vector", lambda E, k=k, w=w: E.tensor_tensor(
            V3(sc1, 0, w), V3(gatesb, k, w), V3(gatesb, 0, w), ALU.is_gt))
        B.emit("vector", lambda E, k=k, w=w: E.tensor_tensor(
            V3(cntsb, 0, w), V3(cntsb, 0, w), V3(sc1, 0, w), ALU.add))
        B.emit("vector", lambda E, k=k, w=w: E.tensor_tensor(
            V3(sc1, 0, w), V3(gatesb, 0, w), V3(gatesb, k, w), ALU.is_ge))
        B.emit("vector", lambda E, k=k, w=w: E.tensor_tensor(
            V3(cntsb, k, w), V3(cntsb, k, w), V3(sc1, 0, w), ALU.add))
    evmask = B.inc("vector", lambda E: E.tensor_scalar(
        masksb[:, :], cntsb[:, :], 4.0, None, ALU.is_lt), "s_vgate")


    for o in range(16):
        k = 48 + o
        slotname = ringK[o % 3]
        slot = PS[slotname]
        B.wait("tensor", "s_sc", ev_aqt)
        B.wait("tensor", *slab_ld[k])
        B.use("tensor", slotname)
        ws = slab_slot_ap(k)[1]
        ev = None
        for d in range(16):
            for h in (0, 512):
                fn = MM(slot[:, h:h + 512], ws[:, d * 128:(d + 1) * 128],
                        qt[:, d * T + h: d * T + h + 512], d == 0, d == 15)
                if d == 15 and h == 512:
                    ev = B.inc("tensor", fn, "s_twout")
                else:
                    B.emit("tensor", fn)
        slab_event[k] = ("s_twout", ev)
        pump_slabs(upto_slab=min(k + 1 + (8 if (k >= 64 or k < 20) else NS), NSLAB))
        B.wait("vector", "s_twout", ev)
        aj = acc[:, o * T:(o + 1) * T]
        r = B.inc("vector", lambda E, aj=aj, slot=slot, k=k: E.scalar_tensor_tensor(
            aj, slot[:, :], biascol[:, k:k + 1], aj, ALU.add, ALU.add), "s_vwout")
        B.release(slotname, "s_v", r)
        ev_wout_last = r
        if o == 1:
            B.wait("tensor", "s_v", evmask)
            B.use("tensor", "pT")
            evmt = None
            for nj in range(8):
                fn = lambda E, nj=nj: E.transpose(
                    pT[0:8, nj * 128:(nj + 1) * 128], masksb[:, nj * 8:(nj + 1) * 8], eye[:, :])
                if nj < 7:
                    B.emit("tensor", fn)
                else:
                    evmt = B.inc("tensor", fn, "s_tmt")
            B.wait("vector", "s_tmt", evmt)
            evmTc = B.inc("vector", lambda E: E.tensor_copy(maskT[0:8, :], pT[0:8, :]), "s_vmaskT")
            B.release("pT", "s_v", evmTc)
            B.wait("sync", "s_v", evmTc)
            B.dma("sync", "s_ddm", mbounce.ap()[:, :], maskT[0:8, :])

    if stage <= 3:
        B.wait("sync", "s_v", ev_wout_last)
        return _flush_and_return()


    res_u = {"u0": ("pC", lambda: pC[:, 0:512]),
             "u1": ("pC", lambda: pC[:, 512:1024]),
             "u2": ("pD", lambda: pD[:, 0:512])}
    B.res["u0"] = B.res.get("pC")
    B.res["u1"] = B.res.get("pC")
    B.res["u2"] = B.res.get("pD")

    w1slot = ["pA", "pB"]
    hw_counter = [0]
    for p in range(4):
        mb_ev = [None, None]
        mb_step = [0]

        def emit_maskB_step(p=p, mb_ev=mb_ev, mb_step=mb_step):
            eL = mb_step[0]
            if eL >= 2:
                return
            B.wait("sync", "s_ddm", 16)
            if p > 0 or eL > 0:
                B.wait("sync", "s_tmb", 4 * p + 2 * eL)
            B.dma("sync", "s_mp", maskpair[0:1, :],
                  mbounce.ap()[2 * p + eL:2 * p + eL + 1, :])
            for hh in range(2):
                B.wait("tensor", "s_mp", 16 * (2 * p + eL + 1))
                B.use("tensor", "u2")
                evb = B.inc("tensor", MM(pD[:, 0:512], onesrow[0:1, :],
                                         maskpair[0:1, hh * 512: hh * 512 + 512],
                                         True, True), "s_tmb")
                B.wait("vector", "s_tmb", evb)
                r = B.inc("vector", lambda E, eL=eL, hh=hh: E.tensor_copy(
                    maskB[:, eL * 1024 + hh * 512: eL * 1024 + hh * 512 + 512],
                    pD[:, 0:512]), "s_vmb")
                B.release("u2", "s_v", r)
                mb_ev[eL] = r
            mb_step[0] += 1

        for eL in range(2):
            for h in range(8):
                hw = hw_counter[0]
                k = 64 + 48 * p + 8 * eL + h
                slotname = w1slot[hw % 2]
                slot = PS[slotname]
                B.wait("tensor", *slab_ld[k])
                B.use("tensor", slotname)
                ws = slab_slot_ap(k)[1]
                ev = None
                for hh in (0, 512):
                    for d in range(16):
                        fn = MM(slot[:, hh:hh + 512], ws[:, d * 128:(d + 1) * 128],
                                xnT[:, d * T + hh: d * T + hh + 512], d == 0, d == 15)
                        if d == 15 and hh == 512:
                            ev = B.inc("tensor", fn, "s_tw1")
                        else:
                            B.emit("tensor", fn)
                slab_event[k] = ("s_tw1", ev)
                pump_slabs(upto_slab=min(k + 1 + (8 if (k >= 64 or k < 20) else NS), NSLAB))
                if eL == 0 and h < 2:
                    emit_maskB_step()
                B.wait("vector", "s_tw1", ev)
                B.wait("vector", "s_v", mb_ev[eL])
                if p > 0:
                    B.wait("vector", "s_tw2", 32 * p)
                dst = kt[:, (eL * 8 + h) * T:(eL * 8 + h + 1) * T]
                r = B.inc("vector", lambda E, dst=dst, slot=slot, eL=eL: E.scalar_tensor_tensor(
                    dst, slot[:, :], 0.0, maskB[:, eL * 1024:(eL + 1) * 1024],
                    ALU.max, ALU.mult), "s_vhT")
                B.release(slotname, "s_v", r)
                ev_hT_last = r
                hw_counter[0] += 1
        for g in range(32):
            o, nh = g // 2, g % 2
            uname = ["u0", "u1", "u2"][g % 3]
            psname, uap = res_u[uname]
            k_e0 = 64 + 48 * p + 16 + 2 * o
            k_e1 = k_e0 + 1
            B.wait("tensor", *slab_ld[k_e0])
            B.wait("tensor", *slab_ld[k_e1])
            B.wait("tensor", "s_v", ev_hT_last)
            B.use("tensor", uname)
            first = True
            ev = None
            for eL, kk in ((0, k_e0), (1, k_e1)):
                ws = slab_slot_ap(kk)[1][:, 0:1024]
                for hc in range(8):
                    last = (eL == 1 and hc == 7)
                    fn = MM(uap()[:, :], ws[:, hc * 128:(hc + 1) * 128],
                            kt[:, (eL * 8 + hc) * T + nh * 512:
                               (eL * 8 + hc) * T + nh * 512 + 512],
                            first, last)
                    if last:
                        ev = B.inc("tensor", fn, "s_tw2")
                    else:
                        B.emit("tensor", fn)
                    first = False
            if nh == 1:
                slab_event[k_e0] = ("s_tw2", ev)
                slab_event[k_e1] = ("s_tw2", ev)
                pump_slabs(upto_slab=min(k_e1 + 1 + 8, NSLAB))
            B.wait("vector", "s_tw2", ev)
            aslice = acc[:, o * T + nh * 512: o * T + nh * 512 + 512]
            r = B.inc("vector", lambda E, aslice=aslice, uap=uap: E.tensor_tensor(
                aslice, aslice, uap()[:, :], ALU.add), "s_vw2")
            B.release(uname, "s_v", r)
            if p == 3 and nh == 1:
                B.wait("sync", "s_v", B.sem("s_v")["n"])
                B.dma("sync", "s_out", y_d[o * 128:(o + 1) * 128, :],
                      acc[:, o * T:(o + 1) * T])

    B.wait("sync", "s_out", 16 * 16)
    _materialize(B, nc, ctx)
    return nc


def _materialize(B, nc, ctx):
    from contextlib import ExitStack
    with ExitStack() as outer:
        for name, s in B.sems.items():
            s["h"] = outer.enter_context(nc.semaphore(name))
        block = outer.enter_context(nc.Block())

        @block.sync
        def _(E):
            for t in B.th["sync"]:
                t(E)

        @block.gpsimd
        def _(E):
            for t in B.th["gpsimd"]:
                t(E)

        @block.tensor
        def _(E):
            for t in B.th["tensor"]:
                t(E)

        @block.vector
        def _(E):
            for t in B.th["vector"]:
                t(E)

        @block.scalar
        def _(E):
            for t in B.th["scalar"]:
                t(E)


_NC_CACHE = []


def _get_program():
    if not _NC_CACHE:
        _NC_CACHE.append(build_program())
    return _NC_CACHE[0]


def _host_prep(L0_attn_X, L0_attn_W_in, L0_attn_b_in, L0_attn_W_out, L0_attn_b_out,
               L0_ffn_wg, L0_ffn_w1, L0_ffn_w2):
    f32 = np.float32
    bf = ml_dtypes.bfloat16
    sigma = np.concatenate([np.arange(0, 1024, 2), np.arange(1, 1024, 2),
                            np.arange(1024, 2048)])
    rows_order = np.concatenate([2048 + sigma, 4096 + sigma, sigma])

    Wnew = np.asarray(L0_attn_W_in, f32)[rows_order]
    bnew = np.asarray(L0_attn_b_in, f32)[rows_order]
    Wop = np.asarray(L0_attn_W_out, f32)[:, :]
    Wop = Wop[:, sigma]
    bout = np.asarray(L0_attn_b_out, f32)
    wg = np.ascontiguousarray(np.asarray(L0_ffn_wg, f32))
    w1 = np.asarray(L0_ffn_w1, f32)
    w2 = np.asarray(L0_ffn_w2, f32)

    stream_tiles = PAIRED + [16 + t for t in range(16)] + [32 + t for t in PAIRED]
    slabsA = np.empty((128, D, 128), f32)
    biasstream = np.empty((64, 128), f32)
    for i, t in enumerate(stream_tiles):
        slabsA[i] = Wnew[128 * t:128 * (t + 1), :].T
        biasstream[i] = bnew[128 * t:128 * (t + 1)]
    for o in range(16):
        slabsA[48 + o] = Wop[128 * o:128 * (o + 1), :].T
        biasstream[48 + o] = bout[128 * o:128 * (o + 1)]
    for p in range(4):
        for eL in range(2):
            e = 2 * p + eL
            for h in range(8):
                slabsA[64 + 16 * p + 8 * eL + h] = w1[e, 128 * h:128 * (h + 1), :].T
    slabsB = np.empty((128, 1024, 128), f32)
    for p in range(4):
        for o in range(16):
            for eL in range(2):
                e = 2 * p + eL
                slabsB[32 * p + 2 * o + eL] = w2[e, 128 * o:128 * (o + 1), :].T

    X = np.asarray(L0_attn_X, f32)
    inv = (10000.0 ** (-(np.arange(0, 1024, 2, dtype=np.float64)) / 1024.0))
    in_maps = []
    common = dict(
        wsA=slabsA, wsB=slabsB, bstr=np.ascontiguousarray(biasstream.T), wg=wg,
        eye=np.eye(128, dtype=bf),
        onescol=np.ones((128, 1), bf), onesrow=np.ones((1, 128), bf),
        onesrhs=np.ones((1, 512), bf),
        consts=np.array([[EPS, 1e-12, 0.0, 1.0, -0.5, 1e-24, -1e-24, 0]], f32),
    )
    for c in range(NC):
        b, q = c // 4, c % 4
        pos = np.arange(q * 1024, (q + 1) * 1024, dtype=np.float64)
        ang = pos[None, :] * inv[:, None]
        tblc = np.empty((2, 512, T), bf)
        tblc[0] = np.cos(ang).astype(np.float32).astype(bf)
        tblc[1] = np.sin(ang).astype(np.float32).astype(bf)
        m = dict(common)
        m["xT"] = np.ascontiguousarray(X[b, q * 1024:(q + 1) * 1024, :].T)
        m["tbl"] = tblc
        m["bsel0"] = np.full((128, 1), 1.0 if b == 0 else 0.0, f32)
        m["bsel1"] = np.full((128, 1), 1.0 if b == 1 else 0.0, f32)
        in_maps.append(m)
    return in_maps


def kernel(**inputs):
    in_maps = _host_prep(**inputs)
    nc = _get_program()
    res = run_bass_kernel_spmd(nc, in_maps, core_ids=list(range(NC)))
    out = np.empty((2, 4096, D), np.float32)
    for c in range(NC):
        b, q = c // 4, c % 4
        out[b, q * 1024:(q + 1) * 1024, :] = res.results[c]["y"].T
    return out


if __name__ == "__main__":
    import reference as _ref
    inp = {k: np.asarray(v) for k, v in _ref.setup_inputs().items()}
    got = kernel(**inp)
    want = np.asarray(_ref.reference(**inp))
    num = float(((got - want) ** 2).mean())
    den = float((want ** 2).mean())
    print(f"resid_var_ratio={num / den:.3e} max_abs={np.max(np.abs(got - want)):.3e}")
